# version 46
# baseline (speedup 1.0000x reference)
import contextlib
import numpy as np
import concourse.bass as bass
import concourse.mybir as mybir
from concourse.bass_utils import run_bass_kernel_spmd

F32 = mybir.dt.float32
BF16 = mybir.dt.bfloat16
U8 = mybir.dt.uint8
AF = mybir.ActivationFunctionType
ALU = mybir.AluOpType
AX = mybir.AxisListType

NCORES = 8
D = 1024
C = 8
H = 16
HD = 64
E = 6
R = 16
SMALL = 118
CAT2 = 112
EPS = 1e-5
NB = 4
NBB = 5


class Builder:
    ENGINES = ("sync", "gpsimd", "vector", "scalar", "tensor")

    def __init__(self):
        self.ops = {e: [] for e in self.ENGINES}
        self.lab = {}
        self.seq = 0

    def op(self, eng, fn, waits=(), inc=None, n=1, label=None, key=None):
        w = tuple(x for x in waits if x is not None)
        if key is None:
            key = float(self.seq)
        rec = {"eng": eng, "fn": fn, "waits": w, "inc": inc,
               "n": n if inc is not None else 0, "label": label,
               "key": key, "seq": self.seq}
        self.seq += 1
        if label is not None:
            assert inc is not None and label not in self.lab
            self.lab[label] = rec
        self.ops[eng].append(rec)

    def L(self, name):
        return name if name in self.lab else None

    def resolve(self):
        for e in self.ENGINES:
            self.ops[e].sort(key=lambda r: (r["key"], r["seq"]))
            cnt = {}
            for r in self.ops[e]:
                if r["inc"] is not None:
                    cnt[r["inc"]] = cnt.get(r["inc"], 0) + r["n"]
                    r["val"] = cnt[r["inc"]]
        bad = []
        for e in self.ENGINES:
            for r in self.ops[e]:
                rw = []
                for wname in r["waits"]:
                    if isinstance(wname, tuple):
                        rw.append(wname)
                        continue
                    p = self.lab[wname]
                    if not (p["key"] < r["key"] or
                            (p["key"] == r["key"] and p["seq"] < r["seq"])):
                        bad.append((wname, p["key"], r["label"] or r["seq"], r["key"]))
                    rw.append((p["inc"], p["val"]))
                r["rw"] = rw
        assert not bad, f"wait edges not backward in key order: {bad[:10]}"


def build_program(NT):
    T = NT * 128
    nc = bass.Bass("TRN2", target_bir_lowering=False, debug=False)

    x_in = nc.dram_tensor("x", [T, D], F32, kind="ExternalInput").ap()
    y_out = nc.dram_tensor("y", [T, D], F32, kind="ExternalOutput").ap()
    wqT = nc.dram_tensor("wqT", [D, D], F32, kind="ExternalInput").ap()
    wkT = nc.dram_tensor("wkT", [D, D], F32, kind="ExternalInput").ap()
    wvT = nc.dram_tensor("wvT", [D, D], F32, kind="ExternalInput").ap()
    wbT = nc.dram_tensor("wbT", [D, D], F32, kind="ExternalInput").ap()
    wbd = nc.dram_tensor("wbd", [C, 128, 128], F32, kind="ExternalInput").ap()
    wsT = nc.dram_tensor("wsT", [D, SMALL], F32, kind="ExternalInput").ap()
    w2 = nc.dram_tensor("w2", [CAT2, D], F32, kind="ExternalInput").ap()
    bqr = nc.dram_tensor("bqr", [1, D], F32, kind="ExternalInput").ap()
    bkr = nc.dram_tensor("bkr", [1, D], F32, kind="ExternalInput").ap()
    bvr = nc.dram_tensor("bvr", [1, D], F32, kind="ExternalInput").ap()
    bbr = nc.dram_tensor("bbr", [1, D], F32, kind="ExternalInput").ap()
    b118 = nc.dram_tensor("b118", [1, SMALL], F32, kind="ExternalInput").ap()
    identd = nc.dram_tensor("identd", [128, 128], F32, kind="ExternalInput").ap()
    eq_sp = nc.dram_tensor("eq_sp", [T, D], BF16).ap()
    ek_sp = nc.dram_tensor("ek_sp", [T, D], BF16).ap()
    v_sp = nc.dram_tensor("v_sp", [T, D], F32).ap()
    cc_in = nc.dram_tensor("cc_in", [128, 2], F32).ap()
    cc_out = nc.dram_tensor("cc_out", [NCORES * 128, 2], F32).ap()

    es = contextlib.ExitStack()
    S = lambda n, sh, dt: es.enter_context(nc.sbuf_tensor(n, sh, dt))
    PS = lambda n, sh, dt: es.enter_context(nc.psum_tensor(n, sh, dt))

    s_wq = S("s_wq", [128, C, D], BF16)
    s_wk = S("s_wk", [128, C, D], BF16)
    s_wv = S("s_wv", [128, C, D], BF16)
    s_wb = S("s_wb", [128, C, D], BF16)
    s_wbd = S("s_wbd", [128, C, 128], BF16)
    s_ws = S("s_ws", [128, C, SMALL], BF16)
    s_w2 = S("s_w2", [CAT2, D], BF16)
    s_id = S("s_id", [128, 128], BF16)
    s_onesf = S("s_onesf", [128, 1], F32)

    s_accq = S("s_accq", [128, NT], F32)
    s_acck = S("s_acck", [128, NT], F32)
    s_nrm = S("s_nrm", [128, 2], F32)
    s_ag8 = S("s_ag8", [128, NCORES, 2], F32)
    s_S128 = S("s_S128", [128, 2], F32)
    s_lnS = S("s_lnS", [128, 2], F32)
    s_rn = S("s_rn", [128, 2], F32)
    s_t2 = S("s_t2", [128, 2], F32)
    s_ts = S("s_ts", [128, 1], F32)
    s_scq = S("s_scq", [128, 1], F32)
    s_scqk = S("s_scqk", [128, 1], F32)
    s_lnscqk = S("s_lnscqk", [128, 1], F32)
    s_onebc = S("s_onebc", [128, 1], BF16)
    s_row2b = S("s_row2b", [2, D], BF16)
    s_row2s = S("s_row2s", [2, SMALL], BF16)
    s_eps = S("s_eps", [128, 1], F32)

    def sz(cols, dt):
        return cols * (2 if dt == BF16 else 4)

    bset = [("x", D, F32), ("eq", D, BF16), ("ek", D, BF16), ("v", D, F32),
            ("z", D, F32), ("ao", D, F32),
            ("u2fm", D, BF16),
            ("cat1", SMALL, F32), ("cat2", CAT2, BF16), ("c2fm", 128, BF16),
            ("s", H, F32), ("r", H, F32), ("w", H, F32),
            ("g6", 8, F32), ("t8", 8, F32), ("k6", 8, F32), ("sw", 8, F32),
            ("tot", 1, F32), ("ngm", 1, F32), ("r2", 2, BF16), ("r3", 1, F32), ("r3sq", 1, F32), ("lnr3", 1, F32),
            ("st6", 12, F32), ("st2", 2, F32), ("lnv", 1, F32),
            ("rstd", 1, F32), ("nmr", 1, F32)]
    aset = [("x", D, F32), ("u1", D, BF16), ("u1fm", D, BF16),
            ("qsb", D, BF16), ("ksb", D, BF16), ("qfm", D, BF16),
            ("kfm", D, BF16), ("v", D, F32), ("eq", D, BF16), ("ek", D, BF16),
            ("st6", 12, F32), ("st2", 2, F32), ("lnv", 1, F32),
            ("rstd", 1, F32), ("nmr", 1, F32)]
    per_b = sum(((sz(c, d) + 31) & ~31) for _, c, d in bset)
    ARENA = NBB * (per_b + 64)
    arena = S("arena", [128, ARENA], U8)
    s_dump = arena[:, ARENA - 2048:ARENA].bitcast(BF16)
    s_bqb = arena[:, ARENA - 4096:ARENA - 2048].bitcast(BF16)
    s_bkb = arena[:, ARENA - 6144:ARENA - 4096].bitcast(BF16)
    s_bvb = arena[:, ARENA - 8192:ARENA - 6144].bitcast(BF16)

    class Carver:
        def __init__(self):
            self.off = 0

        def take(self, cols, dt):
            nb = sz(cols, dt)
            off = (self.off + 31) & ~31
            view = arena[:, off:off + nb].bitcast(dt)
            self.off = off + nb
            assert self.off <= ARENA
            return view

    cv = Carver()
    A = {name: [] for name, _, _ in aset}
    for i in range(NB):
        for name, cols, dt in aset:
            A[name].append(cv.take(cols, dt))
    assert cv.off <= ARENA - 8192, (cv.off, ARENA)
    cv = Carver()
    B = {name: [] for name, _, _ in bset}
    for i in range(NBB):
        for name, cols, dt in bset:
            ap = cv.take(cols, dt)
            if name == "c2fm":
                ap = ap[0:CAT2, :]
            B[name].append(ap)

    a_x, a_u1, a_u1fm = A["x"], A["u1"], A["u1fm"]
    a_qsb, a_ksb, a_qfm, a_kfm = A["qsb"], A["ksb"], A["qfm"], A["kfm"]
    a_v, a_eq, a_ek = A["v"], A["eq"], A["ek"]
    a_st6, a_st2, a_lnv, a_rstd, a_nmr = A["st6"], A["st2"], A["lnv"], A["rstd"], A["nmr"]
    b_x, b_eq, b_ek, b_v = B["x"], B["eq"], B["ek"], B["v"]
    b_z, b_ao = B["z"], B["ao"]
    b_e = B["eq"]
    b_x1 = b_v
    b_u2 = b_ek
    b_u2fm = B["u2fm"]
    b_cat1, b_cat2, b_c2fm = B["cat1"], B["cat2"], B["c2fm"]
    b_s, b_r, b_w = B["s"], B["r"], B["w"]
    b_g6, b_t8, b_k6, b_sw = B["g6"], B["t8"], B["k6"], B["sw"]
    b_tot, b_ngm, b_r2, b_r3 = B["tot"], B["ngm"], B["r2"], B["r3"]
    b_r3sq, b_lnr3 = B["r3sq"], B["lnr3"]
    b_st6, b_st2, b_lnv, b_rstd, b_nmr = B["st6"], B["st2"], B["lnv"], B["rstd"], B["nmr"]

    b_r2T = [S(f"b_r2T{i}", [2, 128], BF16) for i in range(NBB)]

    pQ = PS("pQ", [128, D], F32)
    pK = PS("pK", [128, D], F32)
    pV = PS("pV", [128, D], F32)
    pT6 = PS("pT6", [128, D], BF16)
    pT7 = PS("pT7", [128, D], BF16)

    sem_names = (["sA_x%d" % i for i in range(NB)] +
                 ["sA_sp%d" % i for i in range(NB)] +
                 ["sB_in%d" % i for i in range(NBB)] +
                 ["sB_v%d" % i for i in range(NBB)] +
                 ["sB_out%d" % i for i in range(NBB)] +
                 ["sWa", "sWid", "sWq", "sWk", "sWv", "sWqb", "sWkb", "sWvb",
                  "sWffn", "sCC", "sCCd", "sCCg",
                  "sV", "sS", "sT", "sP"])
    sems = {n: es.enter_context(nc.semaphore(n)) for n in sem_names}

    b = Builder()
    L = b.L

    wk = [-100]
    def wload(name, dst, src, sem, label=None):
        wk[0] += 1
        b.op("gpsimd", lambda g, d=dst, s=src: g.dma_start(out=d, in_=s),
             inc=sem, n=16, label=label, key=wk[0])

    wload("w_id", s_id[:], identd, "sWid", label="w_id")
    wload("w_wq", s_wq[:], wqT.rearrange("(c p) o -> p c o", p=128), "sWq", label="w_wq")
    wload("w_wk", s_wk[:], wkT.rearrange("(c p) o -> p c o", p=128), "sWk", label="w_wk")
    wload("w_wv", s_wv[:], wvT.rearrange("(c p) o -> p c o", p=128), "sWv", label="w_wv")
    wload("w_wbd", s_wbd[:], wbd.rearrange("c p m -> p c m"), "sWa", label="w_a")
    wload("w_bqb", s_bqb[:], bqr.broadcast_to([128, D]), "sWqb", label="w_bqb")
    wload("w_bkb", s_bkb[:], bkr.broadcast_to([128, D]), "sWkb", label="w_bkb")
    wload("w_bvb", s_bvb[:], bvr.broadcast_to([128, D]), "sWvb", label="w_bvb")
    wload("w_wb", s_wb[:], wbT.rearrange("(c p) o -> p c o", p=128), "sWffn")
    wload("w_ws", s_ws[:], wsT.rearrange("(c p) j -> p c j", p=128), "sWffn")
    wload("w_w2", s_w2[:], w2, "sWffn", label="w_ffn")

    b.op("vector", lambda v: v.memset(s_eps[:], EPS), inc="sV", label="v_eps", key=-79)
    b.op("vector", lambda v: v.memset(s_onesf[:], 1.0), inc="sV", label="v_onesf", key=-78)
    b.op("vector", lambda v: v.memset(s_onebc[:], 1.0), inc="sV", label="v_onebc", key=-77.5)
    b.op("vector", lambda v: v.memset(s_nrm[:], 0.0), inc="sV", label="v_z0", key=-77.4)
    b.op("sync", lambda s: s.dma_start(out=cc_in, in_=s_nrm[:]),
         waits=["v_z0"], inc="sCCd", n=16, label="g_ccin0", key=-77.2)
    b.op("gpsimd", lambda g: g.collective_compute(
            "AllGather", ALU.bypass, replica_groups=[list(range(NCORES))],
            ins=[cc_in.opt()], outs=[cc_out.opt()]),
         waits=["g_ccin0"], inc="sCC", n=1, label="g_ccw", key=-77)
    b.op("gpsimd", lambda g: g.dma_start(out=s_row2b[1:2, :], in_=bbr),
         inc="sWffn", n=16, label="w_r2b", key=-76)
    b.op("gpsimd", lambda g: g.dma_start(out=s_row2s[1:2, :], in_=b118),
         inc="sWffn", n=16, label="w_r2s", key=-75)

    IIA = 20.0
    for t in range(NT):
        p = t % NB
        rs = slice(t * 128, (t + 1) * 128)
        ka = lambda s, t=t: t * IIA + s

        b.op("sync", lambda s, p=p, rs=rs: s.dma_start(out=a_x[p][:], in_=x_in[rs, :]),
             waits=[L(f"s_u1n{t-NB}"), L(f"v_bn1_{t-NB}")],
             inc=f"sA_x{p}", n=16, label=f"ax{t}", key=ka(-25.0))

        b.op("vector", lambda v, p=p: v.bn_stats(a_st6[p][:, 0:6], a_x[p][:, 0:512]),
             waits=[f"ax{t}", L(f"v_agg1_{t-NB}")], inc="sV", key=ka(-4.0))
        b.op("vector", lambda v, p=p: v.bn_stats(a_st6[p][:, 6:12], a_x[p][:, 512:1024]),
             waits=[], inc="sV", label=f"v_bn1_{t}", key=ka(-3.9))
        b.op("vector", lambda v, p=p: v.bn_aggr(a_st2[p][:], a_st6[p][:]),
             waits=[f"v_bn1_{t}", L(f"s_ln1_{t-NB}")], inc="sV", label=f"v_agg1_{t}", key=ka(-3.8))
        b.op("scalar", lambda s, p=p: s.activation(a_lnv[p][:], a_st2[p][:, 1:2], AF.Ln, bias=s_eps[:]),
             waits=[f"v_agg1_{t}", L(f"s_rstd1_{t-NB}"), "v_eps"],
             inc="sS", label=f"s_ln1_{t}", key=ka(-3.5))
        b.op("scalar", lambda s, p=p: s.activation(a_rstd[p][:], a_lnv[p][:], AF.Exp, scale=-0.5),
             waits=[f"s_ln1_{t}", L(f"v_nmr1_{t-NB}"), L(f"s_u1n{t-NB}")],
             inc="sS", label=f"s_rstd1_{t}", key=ka(-3.3))
        b.op("vector", lambda v, p=p: v.tensor_scalar(
                a_nmr[p][:], a_st2[p][:, 0:1], a_rstd[p][:], -1.0, ALU.mult, ALU.mult),
             waits=[f"s_rstd1_{t}", L(f"s_u1n{t-NB}")], inc="sV", label=f"v_nmr1_{t}", key=ka(-3.1))
        b.op("scalar", lambda s, p=p: s.activation(
                a_u1[p][:], a_x[p][:], AF.Identity, bias=a_nmr[p][:], scale=a_rstd[p][:]),
             waits=[f"v_nmr1_{t}", L(f"t_Tu1_{t-NB}")], inc="sS", label=f"s_u1n{t}", key=ka(-2.9))

        def tu1(te, p=p):
            for c in range(C):
                ins = te.transpose(pT6[:, c * 128:(c + 1) * 128],
                                   a_u1[p][:, c * 128:(c + 1) * 128], s_id[:])
            return ins
        b.op("tensor", tu1, waits=[f"s_u1n{t}", L(f"s_u1fm_{t-1}"), "w_id"],
             inc="sT", label=f"t_Tu1_{t}", key=ka(7.0))
        b.op("scalar", lambda s, p=p: s.copy(a_u1fm[p][:], pT6[:]),
             waits=[f"t_Tu1_{t}", L(f"t_v_{t-NB}")], inc="sS", label=f"s_u1fm_{t}", key=ka(8.5))

        def proj(te, pdst, wsb, p=p):
            for h in range(2):
                cs = slice(h * 512, (h + 1) * 512)
                for c in range(C):
                    ins = te.matmul(pdst[:, cs], a_u1fm[p][:, c * 128:(c + 1) * 128],
                                    wsb[:, c, cs], start=(c == 0), stop=(c == C - 1),
                                    skip_group_check=True)
            return ins
        b.op("tensor", lambda te, p=p: proj(te, pQ[:], s_wq, p),
             waits=[f"s_u1fm_{t}", L(f"s_qev_{t-1}"), "w_wq"],
             inc="sT", label=f"t_q_{t}", key=ka(10.0))
        b.op("tensor", lambda te, p=p: proj(te, pK[:], s_wk, p),
             waits=[L(f"s_ekev_{t-1}"), "w_wk"], inc="sT", label=f"t_k_{t}", key=ka(11.5))
        b.op("tensor", lambda te, p=p: proj(te, pV[:], s_wv, p),
             waits=[L(f"s_eqev_{t-1}"), "w_wv"], inc="sT", label=f"t_v_{t}", key=ka(13.5))

        b.op("vector", lambda v, p=p: v.tensor_add(a_qsb[p][:], pQ[:], s_bqb[:]),
             waits=[f"t_q_{t}", "w_bqb", L(f"v_ssq_{t-NB}"), L(f"t_Tq_{t-NB}")],
             inc="sV", label=f"s_qev_{t}", key=ka(12.0))
        b.op("vector", lambda v, p=p: v.tensor_add(a_ksb[p][:], pK[:], s_bkb[:]),
             waits=[f"t_k_{t}", "w_bkb", L(f"v_ssk_{t-NB}"), L(f"t_Tk_{t-NB}")],
             inc="sV", label=f"s_kev_{t}", key=ka(13.5))
        b.op("vector", lambda v, p=p: v.tensor_add(a_v[p][:], pV[:], s_bvb[:]),
             waits=[f"t_v_{t}", "w_bvb", L(f"g_v_{t-NB}")], inc="sV", label=f"v_vev_{t}", key=ka(15.0))

        b.op("vector", lambda v, p=p, t=t: v.scalar_tensor_tensor(
                s_dump[:], a_qsb[p][:], 1.0, a_qsb[p][:], ALU.bypass, ALU.mult,
                accum_out=s_accq[:, t:t + 1]),
             waits=[f"s_qev_{t}", f"v_vev_{t}"], inc="sV", label=f"v_ssq_{t}", key=ka(15.5))
        b.op("vector", lambda v, p=p, t=t: v.scalar_tensor_tensor(
                s_dump[:], a_ksb[p][:], 1.0, a_ksb[p][:], ALU.bypass, ALU.mult,
                accum_out=s_acck[:, t:t + 1]),
             waits=[f"s_kev_{t}", f"v_ssq_{t}"], inc="sV", label=f"v_ssk_{t}", key=ka(15.8))

        def tq(te, p=p):
            for c in range(C):
                ins = te.transpose(pT7[:, c * 128:(c + 1) * 128],
                                   a_qsb[p][:, c * 128:(c + 1) * 128], s_id[:])
            return ins
        b.op("tensor", tq, waits=[f"s_qev_{t}", L(f"s_kfm_{t-1}")],
             inc="sT", label=f"t_Tq_{t}", key=ka(13.0))
        def tk(te, p=p):
            for c in range(C):
                ins = te.transpose(pT7[:, c * 128:(c + 1) * 128],
                                   a_ksb[p][:, c * 128:(c + 1) * 128], s_id[:])
            return ins
        b.op("tensor", tk, waits=[f"s_kev_{t}", f"s_qfm_{t}"],
             inc="sT", label=f"t_Tk_{t}", key=ka(14.2))
        b.op("scalar", lambda s, p=p: s.copy(a_qfm[p][:], pT7[:]),
             waits=[f"t_Tq_{t}", L(f"t_eq_{t-NB}")], inc="sS", label=f"s_qfm_{t}", key=ka(13.4))
        b.op("scalar", lambda s, p=p: s.copy(a_kfm[p][:], pT7[:]),
             waits=[f"t_Tk_{t}", L(f"t_ek_{t-NB}")], inc="sS", label=f"s_kfm_{t}", key=ka(14.6))

        def eqmm(te, src, pdst, p=p):
            for c in range(C):
                ins = te.matmul(pdst[:, c * 128:(c + 1) * 128],
                                src[p][:, c * 128:(c + 1) * 128],
                                s_wbd[:, c, :], start=(c % 4 == 0),
                                stop=(c % 4 == 3), skip_group_check=True)
            return ins
        b.op("tensor", lambda te, p=p: eqmm(te, a_qfm, pV[:], p),
             waits=[f"s_qfm_{t}", f"v_vev_{t}", "w_a"], inc="sT", label=f"t_eq_{t}", key=ka(27.3))
        b.op("tensor", lambda te, p=p: eqmm(te, a_kfm, pK[:], p),
             waits=[f"s_kfm_{t}", f"s_kev_{t}"], inc="sT", label=f"t_ek_{t}", key=ka(27.7))
        b.op("scalar", lambda s, p=p: s.copy(a_eq[p][:], pV[:]),
             waits=[f"t_eq_{t}", L(f"g_v_{t-NB}")], inc="sS", label=f"s_eqev_{t}", key=ka(28.6))
        b.op("scalar", lambda s, p=p: s.copy(a_ek[p][:], pK[:]),
             waits=[f"t_ek_{t}", L(f"g_v_{t-NB}")], inc="sS", label=f"s_ekev_{t}", key=ka(28.9))

        b.op("gpsimd", lambda g, p=p, rs=rs: g.dma_start(out=eq_sp[rs, :], in_=a_eq[p][:]),
             waits=[f"s_eqev_{t}"], inc=f"sA_sp{p}", n=16, label=f"g_eq_{t}", key=ka(29.45))
        b.op("gpsimd", lambda g, p=p, rs=rs: g.dma_start(out=ek_sp[rs, :], in_=a_ek[p][:]),
             waits=[f"s_ekev_{t}"], inc=f"sA_sp{p}", n=16, label=f"g_ek_{t}", key=ka(29.55))
        b.op("gpsimd", lambda g, p=p, rs=rs: g.dma_start(out=v_sp[rs, :], in_=a_v[p][:]),
             waits=[f"v_vev_{t}"], inc=f"sA_sp{p}", n=16, label=f"g_v_{t}", key=ka(29.8))

    TL = NT - 1
    FZ = NT * IIA + 25.0
    b.op("vector", lambda v: v.reduce_sum(s_nrm[:, 0:1], s_accq[:], axis=AX.X),
         waits=[f"v_ssq_{TL}"], inc="sV", label="vf_q", key=FZ + 0)
    b.op("vector", lambda v: v.reduce_sum(s_nrm[:, 1:2], s_acck[:], axis=AX.X),
         waits=[f"v_ssk_{TL}"], inc="sV", label="vf_k", key=FZ + 1)
    b.op("sync", lambda g: g.dma_start(out=cc_in, in_=s_nrm[:]),
         waits=["vf_k", "g_ccw"], inc="sCCd", n=16, label="g_ccin", key=FZ + 2)
    b.op("gpsimd", lambda g: g.collective_compute(
            "AllGather", ALU.bypass, replica_groups=[list(range(NCORES))],
            ins=[cc_in.opt()], outs=[cc_out.opt()]),
         waits=["g_ccin"], inc="sCC", n=1, label="g_cc", key=FZ + 3)
    for r in range(NCORES):
        b.op("gpsimd", lambda g, r=r: g.dma_start(
                out=s_ag8[:, r, :], in_=cc_out[r * 128:(r + 1) * 128, :]),
             waits=["g_cc"], inc="sCCg", n=16,
             label=f"g_ag{r}", key=FZ + 4 + r * 0.01)
    b.op("vector", lambda v: v.reduce_sum(
            s_S128[:], s_ag8[:].rearrange("p r c -> p c r"), axis=AX.X),
         waits=[f"g_ag{NCORES-1}"], inc="sV", label="v_S128", key=FZ + 5)
    b.op("scalar", lambda s: s.activation(s_lnS[:], s_S128[:], AF.Ln),
         waits=["v_S128"], inc="sS", label="s_lnS", key=FZ + 14)
    b.op("scalar", lambda s: s.activation(s_rn[:], s_lnS[:], AF.Exp, scale=-0.5),
         waits=["s_lnS"], inc="sS", label="s_rn", key=FZ + 15)
    b.op("vector", lambda v: v.tensor_mul(s_t2[:], s_S128[:], s_rn[:]),
         waits=["s_rn"], inc="sV", label="v_t2a", key=FZ + 16)
    b.op("vector", lambda v: v.tensor_mul(s_t2[:], s_t2[:], s_rn[:]),
         waits=["v_t2a"], inc="sV", label="v_t2b", key=FZ + 17)
    b.op("vector", lambda v: v.reduce_sum(s_ts[:], s_t2[:], axis=AX.X),
         waits=["v_t2b"], inc="sV", label="v_ts", key=FZ + 18)
    b.op("scalar", lambda s: s.activation(s_scq[:], s_ts[:], AF.Exp, scale=-0.5),
         waits=["v_ts"], inc="sS", label="s_scq", key=FZ + 19)
    b.op("vector", lambda v: v.tensor_scalar(s_scqk[:], s_scq[:], 1.0 / 64.0, None, ALU.mult),
         waits=["s_scq"], inc="sV", label="v_scqk", key=FZ + 20)
    b.op("scalar", lambda s: s.activation(s_lnscqk[:], s_scqk[:], AF.Ln),
         waits=["v_scqk"], inc="sS", label="s_lnscqk", key=FZ + 20.5)

    def csb(te):
        for h in range(2):
            cs = slice(h * 512, (h + 1) * 512)
            for c in range(C):
                ins = te.matmul(pQ[0:1, cs], s_onebc[:], s_wb[:, c, cs],
                                start=(c == 0), stop=(c == C - 1),
                                skip_group_check=True)
        return ins
    b.op("tensor", csb, waits=[f"s_qev_{TL}", "w_r2s", "v_onebc"],
         inc="sT", label="t_csb", key=FZ + 22)
    def css(te):
        for c in range(C):
            ins = te.matmul(pK[0:1, 0:SMALL], s_onebc[:], s_ws[:, c, :],
                            start=(c == 0), stop=(c == C - 1),
                            skip_group_check=True)
        return ins
    b.op("tensor", css, waits=[f"s_ekev_{TL}", "w_r2s"],
         inc="sT", label="t_css", key=FZ + 22.2)
    b.op("scalar", lambda s: s.copy(s_row2b[0:1, :], pQ[0:1, :]),
         waits=["t_csb"], inc="sS", label="s_csbev", key=FZ + 22.5)
    b.op("scalar", lambda s: s.copy(s_row2s[0:1, :], pK[0:1, 0:SMALL]),
         waits=["t_css"], inc="sS", label="s_cssev", key=FZ + 23.5)

    adone = ["vf_k", f"s_ekev_{TL}", f"t_ek_{TL}"] + \
            [L(f"g_v_{TL - i}") for i in range(min(NB, NT))]
    for i in range(NBB):
        b.op("vector", lambda v, i=i: v.memset(b_g6[i][:], -1e30),
             waits=adone, inc="sV", label=f"v_mg{i}", key=FZ + 21 + i * 0.1)

    IIB = 11.0
    BZ = FZ + 30.0
    for t in range(NT):
        p = t % NBB
        pp = t & 1
        kb = lambda s, t=t: BZ + t * IIB + s
        rs = slice(t * 128, (t + 1) * 128)
        pF = pQ if pp == 0 else pK
        pSm = pV[:, 0:SMALL] if pp == 0 else pV[:, 512:512 + SMALL]
        pT = pT6 if pp == 0 else pT7
        first = adone if t < NBB else []

        b.op("sync", lambda s, p=p, rs=rs: s.dma_start(out=b_x[p][:], in_=x_in[rs, :]),
             waits=[L(f"g_out_{t-NBB}")] + first, inc=f"sB_in{p}", n=16, label=f"b_lx_{t}", key=kb(6.4))
        b.op("sync", lambda s, p=p, rs=rs: s.dma_start(out=b_eq[p][:], in_=eq_sp[rs, :]),
             waits=[f"g_v_{t}", L(f"v_sraw_{t-NBB}")] + first, inc=f"sB_in{p}", n=16, label=f"b_leq_{t}", key=kb(6.5))
        b.op("sync", lambda s, p=p, rs=rs: s.dma_start(out=b_ek[p][:], in_=ek_sp[rs, :]),
             waits=[L(f"t_Tu2_{t-NBB}")] + first, inc=f"sB_in{p}", n=16, label=f"b_lek_{t}", key=kb(6.6))
        b.op("gpsimd", lambda g, p=p, rs=rs: g.dma_start(out=b_v[p][:], in_=v_sp[rs, :]),
             waits=[f"g_v_{t}", L(f"po_x2_{t-NBB}")] + first, inc=f"sB_v{p}", n=16,
             label=f"b_lv_{t}", key=kb(6.7) if t < NBB else kb(-13.5))

        e12 = b_z[p].bitcast(BF16)
        b.op("scalar", lambda s, p=p, e12=e12: s.activation(
                e12[:, 0:D], b_eq[p][:], AF.Exp, scale=s_rn[:, 0:1],
                bias=s_lnscqk[:]),
             waits=[f"b_lek_{t}", "s_rn", "s_lnscqk", L(f"po_x1_{t-NBB}")],
             inc="sS", label=f"s_e1_{t}", key=kb(8.0))
        b.op("scalar", lambda s, p=p, e12=e12: s.activation(
                e12[:, D:2 * D], b_ek[p][:], AF.Exp, scale=s_rn[:, 1:2]),
             waits=[], inc="sS", label=f"s_e2_{t}", key=kb(8.3))
        b.op("vector", lambda v, p=p, e12=e12: v.tensor_mul(
                b_e[p][:], e12[:, 0:D], e12[:, D:2 * D]),
             waits=[f"s_e2_{t}", L(f"v_sraw_{t-NBB}")], inc="sV", label=f"s_e_{t}", key=kb(15.0))
        b.op("vector", lambda v, p=p: v.reduce_sum(
                b_s[p][:], b_e[p][:].rearrange("t (h m) -> t h m", h=H), axis=AX.X),
             waits=[f"s_e_{t}", L(f"v_w_{t-NBB}")], inc="sV", label=f"v_sraw_{t}", key=kb(16.5))
        b.op("vector", lambda v, p=p: v.reciprocal(b_r[p][:], b_s[p][:]),
             waits=[f"v_sraw_{t}"], inc="sV", label=f"v_recip_{t}", key=kb(17.2))
        b.op("vector", lambda v, p=p: v.tensor_mul(b_w[p][:], b_s[p][:], b_r[p][:]),
             waits=[f"v_recip_{t}"], inc="sV", label=f"v_w_{t}", key=kb(17.4))
        b.op("vector", lambda v, p=p: v.tensor_tensor(
                b_ao[p][:, 0:512].rearrange("t (h m) -> t h m", h=H // 2),
                b_v[p][:, 0:512].rearrange("t (h m) -> t h m", h=H // 2),
                b_w[p][:, 0:8].unsqueeze(2).broadcast_to([128, H // 2, HD]), ALU.mult),
             waits=[f"b_lv_{t}", f"v_w_{t}", L(f"s_ln2o_{t-NBB}"), L(f"v_bn2_{t-NBB}")],
             inc="sV", label=f"po_aoa_{t}", key=kb(18.0))
        b.op("vector", lambda v, p=p: v.tensor_tensor(
                b_ao[p][:, 512:1024].rearrange("t (h m) -> t h m", h=H // 2),
                b_v[p][:, 512:1024].rearrange("t (h m) -> t h m", h=H // 2),
                b_w[p][:, 8:16].unsqueeze(2).broadcast_to([128, H // 2, HD]), ALU.mult),
             waits=[], inc="sV", label=f"po_ao_{t}", key=kb(18.3))

        b.op("vector", lambda v, p=p: v.bn_stats(b_st6[p][:, 0:6], b_ao[p][:, 0:512]),
             waits=[f"po_aoa_{t}", L(f"v_agg4_{t-NBB}")], inc="sV", key=kb(19.0))
        b.op("vector", lambda v, p=p: v.bn_stats(b_st6[p][:, 6:12], b_ao[p][:, 512:1024]),
             waits=[f"po_ao_{t}"], inc="sV", label=f"v_bn2_{t}", key=kb(19.1))
        b.op("vector", lambda v, p=p: v.bn_aggr(b_st2[p][:], b_st6[p][:]),
             waits=[f"v_bn2_{t}", L(f"s_ln4_{t-NBB}")], inc="sV", label=f"v_agg2_{t}", key=kb(19.5))
        b.op("scalar", lambda s, p=p: s.activation(b_lnv[p][:], b_st2[p][:, 1:2], AF.Ln, bias=s_eps[:]),
             waits=[f"v_agg2_{t}", L(f"s_rstd4_{t-NBB}"), "v_eps"],
             inc="sS", label=f"s_ln2_{t}", key=kb(20.0))
        b.op("scalar", lambda s, p=p: s.activation(b_rstd[p][:], b_lnv[p][:], AF.Exp, scale=-0.5),
             waits=[f"s_ln2_{t}", L(f"v_nmr4_{t-NBB}"), L(f"s_ln4o_{t-NBB}")],
             inc="sS", label=f"s_rstd2_{t}", key=kb(20.3))
        b.op("vector", lambda v, p=p: v.tensor_scalar(
                b_nmr[p][:], b_st2[p][:, 0:1], b_rstd[p][:], -1.0, ALU.mult, ALU.mult),
             waits=[f"s_rstd2_{t}", L(f"s_ln4o_{t-NBB}")], inc="sV", label=f"v_nmr2_{t}", key=kb(20.6))
        b.op("scalar", lambda s, p=p: s.activation(
                b_z[p][:, 0:512], b_ao[p][:, 0:512], AF.Identity,
                bias=b_nmr[p][:], scale=b_rstd[p][:]),
             waits=[f"v_nmr2_{t}", f"s_e_{t}", f"s_e2_{t}"], inc="sS",
             label=f"s_ln2oa_{t}", key=kb(21.0))
        b.op("scalar", lambda s, p=p: s.activation(
                b_z[p][:, 512:1024], b_ao[p][:, 512:1024], AF.Identity,
                bias=b_nmr[p][:], scale=b_rstd[p][:]),
             waits=[], inc="sS", label=f"s_ln2o_{t}", key=kb(21.3))
        b.op("gpsimd", lambda g, p=p: g.tensor_add(
                b_x1[p][:, 0:512], b_x[p][:, 0:512], b_z[p][:, 0:512]),
             waits=[f"s_ln2oa_{t}", f"b_lek_{t}", L(f"po_x2_{t-NBB}"),
                    L(f"s_x1c_{t-NBB}"), L(f"v_bn3_{t-NBB}")],
             inc="sP", label=f"po_x1a_{t}", key=kb(22.5))
        b.op("gpsimd", lambda g, p=p: g.tensor_add(
                b_x1[p][:, 512:1024], b_x[p][:, 512:1024], b_z[p][:, 512:1024]),
             waits=[f"s_ln2o_{t}"], inc="sP", label=f"po_x1_{t}", key=kb(22.8))

        b.op("vector", lambda v, p=p: v.bn_stats(b_st6[p][:, 0:6], b_x1[p][:, 0:512]),
             waits=[f"po_x1a_{t}", f"v_agg2_{t}"], inc="sV", key=kb(24.0))
        b.op("vector", lambda v, p=p: v.bn_stats(b_st6[p][:, 6:12], b_x1[p][:, 512:1024]),
             waits=[f"po_x1_{t}"], inc="sV", label=f"v_bn3_{t}", key=kb(24.1))
        b.op("vector", lambda v, p=p: v.bn_aggr(b_st2[p][:], b_st6[p][:]),
             waits=[f"v_bn3_{t}", f"s_ln2_{t}"], inc="sV", label=f"v_agg3_{t}", key=kb(24.5))
        b.op("scalar", lambda s, p=p: s.activation(b_lnv[p][:], b_st2[p][:, 1:2], AF.Ln, bias=s_eps[:]),
             waits=[f"v_agg3_{t}", f"s_rstd2_{t}"], inc="sS", label=f"s_ln3_{t}", key=kb(25.0))
        b.op("scalar", lambda s, p=p: s.activation(b_r3[p][:], b_lnv[p][:], AF.Exp, scale=-0.5),
             waits=[f"s_ln3_{t}", L(f"v_ngs_{t-NBB}"), L(f"s_exg_{t-NBB}")],
             inc="sS", label=f"s_rstd3_{t}", key=kb(25.3))
        b.op("scalar", lambda s, p=p: s.activation(
                b_r2[p][:, 0:1], b_st2[p][:, 0:1], AF.Identity, scale=-1.0),
             waits=[f"s_rstd3_{t}", L(f"t_Trow_{t-NBB}")], inc="sS",
             label=f"s_nmu3_{t}", key=kb(25.5))
        b.op("scalar", lambda s, p=p: s.activation(
                b_r2[p][:, 1:2], b_lnv[p][:], AF.Exp, scale=0.5),
             waits=[f"s_nmu3_{t}"], inc="sS", label=f"s_sg3_{t}", key=kb(25.65))
        b.op("scalar", lambda s, p=p: s.activation(
                b_r3sq[p][:], b_lnv[p][:], AF.Exp, scale=-1.0),
             waits=[f"s_sg3_{t}", L(f"s_ln4_{t-NBB}")], inc="sS",
             label=f"v_r3sq_{t}", key=kb(25.7))
        b.op("scalar", lambda s, p=p: s.activation(
                b_lnr3[p][:], b_lnv[p][:], AF.Identity, scale=-0.5),
             waits=[f"v_r3sq_{t}", L(f"s_rstd4_{t-NBB}")], inc="sS",
             label=f"v_lnr3_{t}", key=kb(25.75))
        b.op("scalar", lambda s, p=p: s.copy(b_u2[p][:], b_x1[p][:]),
             waits=[f"po_x1a_{t}", f"po_x1_{t}", f"s_e2_{t}", f"s_e_{t}"], inc="sS",
             label=f"s_x1c_{t}", key=kb(23.3))

        def tu2(te, p=p, pT=pT):
            for c in range(C):
                ins = te.transpose(pT[:, c * 128:(c + 1) * 128],
                                   b_u2[p][:, c * 128:(c + 1) * 128], s_id[:])
            return ins
        tb1waits = [f"s_x1c_{t}", L(f"s_c2fm_{t-2}"), L(f"s_rowev_{t-1}")]
        if t == 0:
            tb1waits.append(f"s_u1fm_{TL}")
        if t == 1:
            tb1waits.append(f"s_kfm_{TL}")
        b.op("tensor", tu2, waits=tb1waits, inc="sT", label=f"t_Tu2_{t}", key=kb(27.5))
        b.op("vector", lambda v, p=p, pT=pT: v.tensor_scalar(
                b_u2fm[p][:], pT[:], 0.0, None, ALU.add),
             waits=[f"t_Tu2_{t}", L(f"t_base_{t-NBB}")], inc="sV", label=f"s_u2fm_{t}", key=kb(28.5))

        def small1(te, p=p, pSm=pSm):
            for c in range(C):
                ins = te.matmul(pSm, b_u2fm[p][:, c * 128:(c + 1) * 128],
                                s_ws[:, c, :], start=(c == 0), stop=False,
                                skip_group_check=True)
            return ins
        s1waits = [f"s_u2fm_{t}", L(f"s_cat1_{t-2}"), "w_r2s"]
        if t <= 1:
            s1waits.append(f"s_eqev_{TL}")
        b.op("tensor", small1, waits=s1waits, inc="sT", label=f"t_s1_{t}", key=kb(29.5))

        pTo = pT7 if pp == 0 else pT6
        trwaits = [f"s_sg3_{t}", L(f"s_c2fm_{t-1}")]
        if t == 0:
            trwaits.append(f"s_kfm_{TL}")
        b.op("tensor", lambda te, p=p, pTo=pTo: te.transpose(
                pTo[0:2, 0:128], b_r2[p][:], s_id[:]),
             waits=trwaits, inc="sT", label=f"t_Trow_{t}", key=kb(29.6))
        b.op("scalar", lambda s, p=p, pTo=pTo: s.copy(b_r2T[p][:], pTo[0:2, 0:128]),
             waits=[f"t_Trow_{t}", L(f"t_fix_{t-NBB}")], inc="sS",
             label=f"s_rowev_{t}", key=kb(29.7))
        b.op("tensor", lambda te, p=p, pSm=pSm: te.matmul(
                pSm, b_r2T[p][:], s_row2s[:], start=False, stop=True,
                skip_group_check=True),
             waits=[f"s_rowev_{t}", "s_cssev"], inc="sT", label=f"t_s1f_{t}", key=kb(29.8))

        def base(te, p=p, pF=pF):
            for h in range(2):
                cs = slice(h * 512, (h + 1) * 512)
                for c in range(C):
                    ins = te.matmul(pF[:, cs], b_u2fm[p][:, c * 128:(c + 1) * 128],
                                    s_wb[:, c, cs], start=(c == 0), stop=False,
                                    skip_group_check=True)
            return ins
        bwaits = [f"s_u2fm_{t}", L(f"s_ln4o_{t-2}"), "w_r2s"]
        if t == 0:
            bwaits.append("s_csbev")
        if t == 1:
            bwaits.append("s_cssev")
        b.op("tensor", base, waits=bwaits, inc="sT", label=f"t_base_{t}", key=kb(30.0))
        def fix(te, p=p, pF=pF):
            for h in range(2):
                cs = slice(h * 512, (h + 1) * 512)
                ins = te.matmul(pF[:, cs], b_r2T[p][:], s_row2b[:, cs],
                                start=False, stop=False, skip_group_check=True)
            return ins
        b.op("tensor", fix, waits=[f"s_rowev_{t}", "s_csbev"],
             inc="sT", label=f"t_fix_{t}", key=kb(30.2))

        b.op("scalar", lambda s, p=p, pSm=pSm: s.copy(b_cat1[p][:], pSm),
             waits=[f"t_s1f_{t}", L(f"v_shc_{t-NBB}")], inc="sS", label=f"s_cat1_{t}", key=kb(30.5))
        b.op("vector", lambda v, p=p: v.reduce_max(
                b_ngm[p][:], b_cat1[p][:, 112:118], axis=AX.X, negate=True),
             waits=[f"s_cat1_{t}", L(f"s_exg_{t-NBB}")], inc="sV", label=f"v_ngm_{t}", key=kb(31.0))
        b.op("vector", lambda v, p=p: v.tensor_scalar(
                b_ngm[p][:], b_ngm[p][:], b_r3[p][:], None, ALU.mult),
             waits=[f"v_ngm_{t}", f"s_rstd3_{t}"], inc="sV", label=f"v_ngs_{t}", key=kb(31.1))
        b.op("scalar", lambda s, p=p: s.activation(
                b_g6[p][:, 0:6], b_cat1[p][:, 112:118], AF.Exp, bias=b_ngm[p][:],
                scale=b_r3[p][:], accum_out=b_tot[p][:]),
             waits=[f"v_ngs_{t}", L(f"v_gate_{t-NBB}"), L(f"v_top8_{t-NBB}"),
                    L(f"v_rt_{t-NBB}"), f"v_mg{p}"],
             inc="sS", label=f"s_exg_{t}", key=kb(31.3))
        b.op("vector", lambda v, p=p: v.reciprocal(b_ngm[p][:], b_tot[p][:]),
             waits=[f"s_exg_{t}"], inc="sV", label=f"v_grec_{t}", key=kb(31.8))
        b.op("vector", lambda v, p=p: v.tensor_scalar(
                b_g6[p][:, 0:6], b_g6[p][:, 0:6], b_ngm[p][:], None, ALU.mult),
             waits=[f"v_grec_{t}"], inc="sV", label=f"v_gate_{t}", key=kb(32.0))
        b.op("vector", lambda v, p=p: v.max(b_t8[p][:], b_g6[p][:]),
             waits=[f"v_gate_{t}", L(f"v_swt_{t-NBB}")], inc="sV", label=f"v_top8_{t}", key=kb(32.2))
        b.op("vector", lambda v, p=p: v.scalar_tensor_tensor(
                b_sw[p][:, 0:6], b_g6[p][:, 0:6], b_t8[p][:, 1:2],
                b_g6[p][:, 0:6], ALU.is_ge, ALU.mult, accum_out=b_tot[p][:]),
             waits=[f"v_top8_{t}", L(f"v_spw_{t-NBB}")], inc="sV", label=f"v_swt_{t}", key=kb(32.4))
        b.op("vector", lambda v, p=p: v.tensor_scalar(
                b_tot[p][:], b_tot[p][:], 1e-6, None, ALU.add),
             waits=[f"v_swt_{t}"], inc="sV", label=f"v_tot1_{t}", key=kb(33.0))
        b.op("vector", lambda v, p=p: v.reciprocal(b_ngm[p][:], b_tot[p][:]),
             waits=[f"v_tot1_{t}", f"v_gate_{t}"], inc="sV", label=f"v_rt_{t}", key=kb(33.2))
        b.op("vector", lambda v, p=p: v.tensor_scalar(
                b_sw[p][:, 0:6], b_sw[p][:, 0:6], b_ngm[p][:], None, ALU.mult),
             waits=[f"v_rt_{t}"], inc="sV", label=f"v_spw_{t}", key=kb(33.4))
        b.op("gpsimd", lambda g, p=p: g.tensor_tensor(
                b_cat2[p][:, 0:96].rearrange("t (e r) -> t e r", e=E),
                b_cat1[p][:, 0:96].rearrange("t (e r) -> t e r", e=E),
                b_sw[p][:, 0:6].unsqueeze(2).broadcast_to([128, E, R]), ALU.mult),
             waits=[f"v_spw_{t}", L(f"t_Tc2_{t-NBB}")], inc="sP", label=f"v_hsc_{t}", key=kb(33.6))
        b.op("scalar", lambda s, p=p: s.copy(
                b_cat2[p][:, 96:112], b_cat1[p][:, 96:112]),
             waits=[f"v_hsc_{t}", f"s_cat1_{t}"], inc="sS", label=f"v_shc_{t}", key=kb(33.8))

        b.op("tensor", lambda te, p=p, pT=pT: te.transpose(
                pT[0:CAT2, 0:128], b_cat2[p][:], s_id[:]),
             waits=[f"v_shc_{t}", f"s_u2fm_{t}"], inc="sT", label=f"t_Tc2_{t}", key=kb(34.5))
        b.op("vector", lambda v, p=p, pT=pT: v.tensor_scalar(
                b_c2fm[p][:], pT[0:CAT2, 0:128], 0.0, None, ALU.add),
             waits=[f"t_Tc2_{t}", L(f"t_ffn_{t-NBB}")], inc="sV", label=f"s_c2fm_{t}", key=kb(35.0))
        def mm2(te, p=p, pF=pF):
            for h in range(2):
                cs = slice(h * 512, (h + 1) * 512)
                ins = te.matmul(pF[:, cs], b_c2fm[p][:], s_w2[:, cs],
                                start=False, stop=(h == 1), skip_group_check=True)
            return ins
        b.op("tensor", mm2, waits=[f"s_c2fm_{t}", f"t_base_{t}", "w_r2s"],
             inc="sT", label=f"t_ffn_{t}", key=kb(35.5))
        b.op("vector", lambda v, p=p, pF=pF: v.bn_stats(b_st6[p][:, 0:6], pF[:, 0:512]),
             waits=[f"t_ffn_{t}", f"v_agg3_{t}"], inc="sV", key=kb(37.5))
        b.op("vector", lambda v, p=p, pF=pF: v.bn_stats(b_st6[p][:, 6:12], pF[:, 512:1024]),
             waits=[], inc="sV", label=f"v_bn4_{t}", key=kb(37.6))
        b.op("vector", lambda v, p=p: v.bn_aggr(b_st2[p][:], b_st6[p][:]),
             waits=[f"v_bn4_{t}", f"s_ln3_{t}"], inc="sV", label=f"v_agg4_{t}", key=kb(38.0))
        b.op("scalar", lambda s, p=p: s.activation(
                b_lnv[p][:], b_st2[p][:, 1:2], AF.Ln, bias=s_eps[:],
                scale=b_r3sq[p][:]),
             waits=[f"v_agg4_{t}", f"v_r3sq_{t}", f"s_sg3_{t}", f"v_lnr3_{t}"],
             inc="sS", label=f"s_ln4_{t}", key=kb(38.5))
        b.op("scalar", lambda s, p=p: s.activation(
                b_rstd[p][:], b_lnv[p][:], AF.Exp, scale=-0.5, bias=b_lnr3[p][:]),
             waits=[f"s_ln4_{t}", f"v_nmr2_{t}", f"s_ln2o_{t}"],
             inc="sS", label=f"s_rstd4_{t}", key=kb(38.8))
        b.op("vector", lambda v, p=p: v.tensor_scalar(
                b_nmr[p][:], b_st2[p][:, 0:1], b_rstd[p][:], -1.0, ALU.mult, ALU.mult),
             waits=[f"s_rstd4_{t}", f"s_ln2o_{t}"], inc="sV", label=f"v_nmr4_{t}", key=kb(39.1))
        b.op("scalar", lambda s, p=p, pF=pF: s.activation(
                b_z[p][:, 0:512], pF[:, 0:512], AF.Identity,
                bias=b_nmr[p][:], scale=b_rstd[p][:]),
             waits=[f"v_nmr4_{t}", f"po_x1_{t}"], inc="sS", label=f"s_ln4oa_{t}", key=kb(39.5))
        b.op("scalar", lambda s, p=p, pF=pF: s.activation(
                b_z[p][:, 512:1024], pF[:, 512:1024], AF.Identity,
                bias=b_nmr[p][:], scale=b_rstd[p][:]),
             waits=[], inc="sS", label=f"s_ln4o_{t}", key=kb(39.8))
        b.op("gpsimd", lambda g, p=p: g.tensor_add(
                b_x[p][:, 0:512], b_x1[p][:, 0:512], b_z[p][:, 0:512]),
             waits=[f"s_ln4oa_{t}"], inc="sP", label=f"po_x2a_{t}", key=kb(40.5))
        b.op("gpsimd", lambda g, p=p: g.tensor_add(
                b_x[p][:, 512:1024], b_x1[p][:, 512:1024], b_z[p][:, 512:1024]),
             waits=[f"s_ln4o_{t}"], inc="sP", label=f"po_x2_{t}", key=kb(41.0))
        b.op("gpsimd", lambda g, p=p, rs=rs: g.dma_start(out=y_out[rs, :], in_=b_x[p][:]),
             waits=[f"po_x2_{t}"], inc=f"sB_out{p}", n=16, label=f"g_out_{t}", key=kb(42.0))

    def nt_i(i):
        return len([t for t in range(NT) if t % NB == i])
    def nt_b(i):
        return len([t for t in range(NT) if t % NBB == i])
    retire = [("sCCd", 32), ("sCCg", NCORES * 16), ("sCC", 2),
              ("sWa", 16), ("sWid", 16), ("sWq", 16), ("sWk", 16), ("sWv", 16),
              ("sWqb", 16), ("sWkb", 16), ("sWvb", 16), ("sWffn", 80)]
    for i in range(NB):
        retire += [(f"sA_x{i}", nt_i(i) * 16), (f"sA_sp{i}", nt_i(i) * 48)]
    for i in range(NBB):
        retire += [(f"sB_out{i}", nt_b(i) * 16), (f"sB_in{i}", nt_b(i) * 48),
                   (f"sB_v{i}", nt_b(i) * 16)]
    b.op("gpsimd", lambda g: g.nop(), waits=retire, key=1e9)

    b.resolve()
    with nc.Block() as block:
        def mkbody(ename):
            def body(eng):
                last = {}
                for r in b.ops[ename]:
                    for semn, val in r["rw"]:
                        if last.get(semn, 0) < val:
                            eng.wait_ge(sems[semn], val)
                            last[semn] = val
                    ins = r["fn"](eng)
                    if r["inc"] is not None:
                        ins.then_inc(sems[r["inc"]], r["n"])
            return body
        block.sync(mkbody("sync"))
        block.gpsimd(mkbody("gpsimd"))
        block.vector(mkbody("vector"))
        block.scalar(mkbody("scalar"))
        block.tensor(mkbody("tensor"))

    es.close()
    return nc


def _host_layout(inputs):
    f32 = np.float32
    Wq = np.asarray(inputs["L0_attn_Wq"], f32)
    Wk = np.asarray(inputs["L0_attn_Wk"], f32)
    Wv = np.asarray(inputs["L0_attn_Wv"], f32)
    Wb = np.asarray(inputs["L0_ffn_base_W"], f32)
    weights = np.asarray(inputs["L0_attn_weights"], f32)
    shared_A = np.asarray(inputs["L0_ffn_shared_A"], f32)
    shared_B = np.asarray(inputs["L0_ffn_shared_B"], f32)
    routed_A = np.asarray(inputs["L0_ffn_routed_A"], f32)
    routed_B = np.asarray(inputs["L0_ffn_routed_B"], f32)
    router_W = np.asarray(inputs["L0_ffn_router_W"], f32)
    router_b = np.asarray(inputs["L0_ffn_router_b"], f32)

    wm = {}
    wm["wqT"] = np.ascontiguousarray(Wq.T)
    wm["wkT"] = np.ascontiguousarray(Wk.T)
    wm["wvT"] = np.ascontiguousarray(Wv.T)
    wm["wbT"] = np.ascontiguousarray(Wb.T)
    bd = np.zeros((C, 128, 128), f32)
    for c in range(C):
        bd[c, 0:64, 0:64] = weights[2 * c].T
        bd[c, 64:128, 64:128] = weights[2 * c + 1].T
    wm["wbd"] = bd
    wsmall = np.concatenate([routed_A.reshape(96, D), shared_A, router_W], 0)
    wm["wsT"] = np.ascontiguousarray(wsmall.T)
    wm["w2"] = np.ascontiguousarray(
        np.concatenate([routed_B.transpose(0, 2, 1).reshape(96, D), shared_B.T], 0))
    wm["bqr"] = np.ascontiguousarray(inputs["L0_attn_bq"].astype(f32)[None, :])
    wm["bkr"] = np.ascontiguousarray(inputs["L0_attn_bk"].astype(f32)[None, :])
    wm["bvr"] = np.ascontiguousarray(inputs["L0_attn_bv"].astype(f32)[None, :])
    wm["bbr"] = np.ascontiguousarray(inputs["L0_ffn_base_b"].astype(f32)[None, :])
    wm["b118"] = np.ascontiguousarray(
        np.concatenate([np.zeros(CAT2, f32), router_b])[None, :])
    wm["identd"] = np.eye(128, dtype=f32)
    return wm


def kernel(**inputs):
    x = np.asarray(inputs["L0_attn_x"], np.float32)
    wm = _host_layout(inputs)
    nc = build_program(32)
    in_maps = [dict(wm, x=np.ascontiguousarray(x[c])) for c in range(NCORES)]
    res = run_bass_kernel_spmd(nc, in_maps, core_ids=list(range(NCORES)))
    return np.stack([res.results[c]["y"] for c in range(NCORES)], axis=0)


# version 47
# speedup vs baseline: 1.1970x; 1.1970x over previous
import contextlib
import numpy as np
import concourse.bass as bass
import concourse.mybir as mybir
from concourse.bass_utils import run_bass_kernel_spmd

F32 = mybir.dt.float32
BF16 = mybir.dt.bfloat16
U8 = mybir.dt.uint8
AF = mybir.ActivationFunctionType
ALU = mybir.AluOpType
AX = mybir.AxisListType

NCORES = 8
D = 1024
C = 8
H = 16
HD = 64
E = 6
R = 16
SMALL = 118
CAT2 = 112
EPS = 1e-5
NB = 4
NBB = 5


class Builder:
    ENGINES = ("sync", "gpsimd", "vector", "scalar", "tensor")

    def __init__(self):
        self.ops = {e: [] for e in self.ENGINES}
        self.lab = {}
        self.seq = 0

    def op(self, eng, fn, waits=(), inc=None, n=1, label=None, key=None):
        w = tuple(x for x in waits if x is not None)
        if key is None:
            key = float(self.seq)
        rec = {"eng": eng, "fn": fn, "waits": w, "inc": inc,
               "n": n if inc is not None else 0, "label": label,
               "key": key, "seq": self.seq}
        self.seq += 1
        if label is not None:
            assert inc is not None and label not in self.lab
            self.lab[label] = rec
        self.ops[eng].append(rec)

    def L(self, name):
        return name if name in self.lab else None

    def resolve(self):
        for e in self.ENGINES:
            self.ops[e].sort(key=lambda r: (r["key"], r["seq"]))
            cnt = {}
            for r in self.ops[e]:
                if r["inc"] is not None:
                    cnt[r["inc"]] = cnt.get(r["inc"], 0) + r["n"]
                    r["val"] = cnt[r["inc"]]
        bad = []
        for e in self.ENGINES:
            for r in self.ops[e]:
                rw = []
                for wname in r["waits"]:
                    if isinstance(wname, tuple):
                        rw.append(wname)
                        continue
                    p = self.lab[wname]
                    if not (p["key"] < r["key"] or
                            (p["key"] == r["key"] and p["seq"] < r["seq"])):
                        bad.append((wname, p["key"], r["label"] or r["seq"], r["key"]))
                    rw.append((p["inc"], p["val"]))
                r["rw"] = rw
        assert not bad, f"wait edges not backward in key order: {bad[:10]}"


def build_program(NT):
    T = NT * 128
    nc = bass.Bass("TRN2", target_bir_lowering=False, debug=False)

    x_in = nc.dram_tensor("x", [T, D], F32, kind="ExternalInput").ap()
    y_out = nc.dram_tensor("y", [T, D], F32, kind="ExternalOutput").ap()
    wqT = nc.dram_tensor("wqT", [D, D], F32, kind="ExternalInput").ap()
    wkT = nc.dram_tensor("wkT", [D, D], F32, kind="ExternalInput").ap()
    wvT = nc.dram_tensor("wvT", [D, D], F32, kind="ExternalInput").ap()
    wbT = nc.dram_tensor("wbT", [D, D], F32, kind="ExternalInput").ap()
    wbd = nc.dram_tensor("wbd", [C, 128, 128], F32, kind="ExternalInput").ap()
    wsT = nc.dram_tensor("wsT", [D, SMALL], F32, kind="ExternalInput").ap()
    w2 = nc.dram_tensor("w2", [CAT2, D], F32, kind="ExternalInput").ap()
    bqr = nc.dram_tensor("bqr", [1, D], F32, kind="ExternalInput").ap()
    bkr = nc.dram_tensor("bkr", [1, D], F32, kind="ExternalInput").ap()
    bvr = nc.dram_tensor("bvr", [1, D], F32, kind="ExternalInput").ap()
    bbr = nc.dram_tensor("bbr", [1, D], F32, kind="ExternalInput").ap()
    b118 = nc.dram_tensor("b118", [1, SMALL], F32, kind="ExternalInput").ap()
    identd = nc.dram_tensor("identd", [128, 128], F32, kind="ExternalInput").ap()
    eq_sp = nc.dram_tensor("eq_sp", [T, D], BF16).ap()
    ek_sp = nc.dram_tensor("ek_sp", [T, D], BF16).ap()
    v_sp = nc.dram_tensor("v_sp", [T, D], F32).ap()
    cc_in = nc.dram_tensor("cc_in", [128, 2], F32).ap()
    cc_out = nc.dram_tensor("cc_out", [NCORES * 128, 2], F32).ap()

    es = contextlib.ExitStack()
    S = lambda n, sh, dt: es.enter_context(nc.sbuf_tensor(n, sh, dt))
    PS = lambda n, sh, dt: es.enter_context(nc.psum_tensor(n, sh, dt))

    s_wq = S("s_wq", [128, C, D], BF16)
    s_wk = S("s_wk", [128, C, D], BF16)
    s_wv = S("s_wv", [128, C, D], BF16)
    s_wb = S("s_wb", [128, C, D], BF16)
    s_wbd = S("s_wbd", [128, C, 128], BF16)
    s_ws = S("s_ws", [128, C, SMALL], BF16)
    s_w2 = S("s_w2", [CAT2, D], BF16)
    s_id = S("s_id", [128, 128], BF16)
    s_onesf = S("s_onesf", [128, 1], F32)

    s_accq = S("s_accq", [128, NT], F32)
    s_acck = S("s_acck", [128, NT], F32)
    s_nrm = S("s_nrm", [128, 2], F32)
    s_ag8 = S("s_ag8", [128, NCORES, 2], F32)
    s_S128 = S("s_S128", [128, 2], F32)
    s_lnS = S("s_lnS", [128, 2], F32)
    s_rn = S("s_rn", [128, 2], F32)
    s_t2 = S("s_t2", [128, 2], F32)
    s_ts = S("s_ts", [128, 1], F32)
    s_scq = S("s_scq", [128, 1], F32)
    s_scqk = S("s_scqk", [128, 1], F32)
    s_lnscqk = S("s_lnscqk", [128, 1], F32)
    s_onebc = S("s_onebc", [128, 1], BF16)
    s_row2b = S("s_row2b", [2, D], BF16)
    s_row2s = S("s_row2s", [2, SMALL], BF16)
    s_eps = S("s_eps", [128, 1], F32)

    def sz(cols, dt):
        return cols * (2 if dt == BF16 else 4)

    bset = [("x", D, F32), ("eq", D, BF16), ("ek", D, BF16), ("v", D, F32),
            ("z", D, F32), ("ao", D, F32),
            ("u2fm", D, BF16),
            ("cat1", SMALL, F32), ("cat2", CAT2, BF16), ("c2fm", 128, BF16),
            ("s", H, F32), ("r", H, F32), ("w", H, F32),
            ("g6", 8, F32), ("t8", 8, F32), ("k6", 8, F32), ("sw", 8, F32),
            ("tot", 1, F32), ("ngm", 1, F32), ("r2", 2, BF16), ("r3", 1, F32), ("r3sq", 1, F32), ("lnr3", 1, F32),
            ("st6", 12, F32), ("st2", 2, F32), ("lnv", 1, F32),
            ("rstd", 1, F32), ("nmr", 1, F32)]
    aset = [("x", D, F32), ("u1", D, BF16), ("u1fm", D, BF16),
            ("qsb", D, BF16), ("ksb", D, BF16), ("qfm", D, BF16),
            ("kfm", D, BF16), ("v", D, F32), ("eq", D, BF16), ("ek", D, BF16),
            ("st6", 12, F32), ("st2", 2, F32), ("lnv", 1, F32),
            ("rstd", 1, F32), ("nmr", 1, F32)]
    per_b = sum(((sz(c, d) + 31) & ~31) for _, c, d in bset)
    ARENA = NBB * (per_b + 64)
    arena = S("arena", [128, ARENA], U8)
    s_dump = arena[:, ARENA - 2048:ARENA].bitcast(BF16)
    s_bqb = arena[:, ARENA - 4096:ARENA - 2048].bitcast(BF16)
    s_bkb = arena[:, ARENA - 6144:ARENA - 4096].bitcast(BF16)
    s_bvb = arena[:, ARENA - 8192:ARENA - 6144].bitcast(BF16)

    class Carver:
        def __init__(self):
            self.off = 0

        def take(self, cols, dt):
            nb = sz(cols, dt)
            off = (self.off + 31) & ~31
            view = arena[:, off:off + nb].bitcast(dt)
            self.off = off + nb
            assert self.off <= ARENA
            return view

    cv = Carver()
    A = {name: [] for name, _, _ in aset}
    for i in range(NB):
        for name, cols, dt in aset:
            A[name].append(cv.take(cols, dt))
    assert cv.off <= ARENA - 8192, (cv.off, ARENA)
    cv = Carver()
    B = {name: [] for name, _, _ in bset}
    for i in range(NBB):
        for name, cols, dt in bset:
            ap = cv.take(cols, dt)
            if name == "c2fm":
                ap = ap[0:CAT2, :]
            B[name].append(ap)

    a_x, a_u1, a_u1fm = A["x"], A["u1"], A["u1fm"]
    a_qsb, a_ksb, a_qfm, a_kfm = A["qsb"], A["ksb"], A["qfm"], A["kfm"]
    a_v, a_eq, a_ek = A["v"], A["eq"], A["ek"]
    a_st6, a_st2, a_lnv, a_rstd, a_nmr = A["st6"], A["st2"], A["lnv"], A["rstd"], A["nmr"]
    b_x, b_eq, b_ek, b_v = B["x"], B["eq"], B["ek"], B["v"]
    b_z, b_ao = B["z"], B["ao"]
    b_e = B["eq"]
    b_x1 = b_v
    b_u2 = b_ek
    b_u2fm = B["u2fm"]
    b_cat1, b_cat2, b_c2fm = B["cat1"], B["cat2"], B["c2fm"]
    b_s, b_r, b_w = B["s"], B["r"], B["w"]
    b_g6, b_t8, b_k6, b_sw = B["g6"], B["t8"], B["k6"], B["sw"]
    b_tot, b_ngm, b_r2, b_r3 = B["tot"], B["ngm"], B["r2"], B["r3"]
    b_r3sq, b_lnr3 = B["r3sq"], B["lnr3"]
    b_st6, b_st2, b_lnv, b_rstd, b_nmr = B["st6"], B["st2"], B["lnv"], B["rstd"], B["nmr"]

    b_r2T = [S(f"b_r2T{i}", [2, 128], BF16) for i in range(NBB)]

    pQ = PS("pQ", [128, D], F32)
    pK = PS("pK", [128, D], F32)
    pV = PS("pV", [128, D], F32)
    pT6 = PS("pT6", [128, D], BF16)
    pT7 = PS("pT7", [128, D], BF16)

    sem_names = (["sA_x%d" % i for i in range(NB)] +
                 ["sA_sp%d" % i for i in range(NB)] +
                 ["sB_in%d" % i for i in range(NBB)] +
                 ["sB_v%d" % i for i in range(NBB)] +
                 ["sB_out%d" % i for i in range(NBB)] +
                 ["sWa", "sWid", "sWq", "sWk", "sWv", "sWqb", "sWkb", "sWvb",
                  "sWffn", "sCC", "sCCd", "sCCg",
                  "sV", "sS", "sT", "sP"])
    sems = {n: es.enter_context(nc.semaphore(n)) for n in sem_names}

    b = Builder()
    L = b.L

    wk = [-100]
    def wload(name, dst, src, sem, label=None):
        wk[0] += 1
        b.op("gpsimd", lambda g, d=dst, s=src: g.dma_start(out=d, in_=s),
             inc=sem, n=16, label=label, key=wk[0])

    wload("w_id", s_id[:], identd, "sWid", label="w_id")
    wload("w_wq", s_wq[:], wqT.rearrange("(c p) o -> p c o", p=128), "sWq", label="w_wq")
    wload("w_wk", s_wk[:], wkT.rearrange("(c p) o -> p c o", p=128), "sWk", label="w_wk")
    wload("w_wv", s_wv[:], wvT.rearrange("(c p) o -> p c o", p=128), "sWv", label="w_wv")
    wload("w_wbd", s_wbd[:], wbd.rearrange("c p m -> p c m"), "sWa", label="w_a")
    wload("w_bqb", s_bqb[:], bqr.broadcast_to([128, D]), "sWqb", label="w_bqb")
    wload("w_bkb", s_bkb[:], bkr.broadcast_to([128, D]), "sWkb", label="w_bkb")
    wload("w_bvb", s_bvb[:], bvr.broadcast_to([128, D]), "sWvb", label="w_bvb")
    wload("w_wb", s_wb[:], wbT.rearrange("(c p) o -> p c o", p=128), "sWffn")
    wload("w_ws", s_ws[:], wsT.rearrange("(c p) j -> p c j", p=128), "sWffn")
    wload("w_w2", s_w2[:], w2, "sWffn", label="w_ffn")

    b.op("vector", lambda v: v.memset(s_eps[:], EPS), inc="sV", label="v_eps", key=-79)
    b.op("vector", lambda v: v.memset(s_onesf[:], 1.0), inc="sV", label="v_onesf", key=-78)
    b.op("vector", lambda v: v.memset(s_onebc[:], 1.0), inc="sV", label="v_onebc", key=-77.5)
    b.op("vector", lambda v: v.memset(s_nrm[:], 0.0), inc="sV", label="v_z0", key=-77.4)
    b.op("sync", lambda s: s.dma_start(out=cc_in, in_=s_nrm[:]),
         waits=["v_z0"], inc="sCCd", n=16, label="g_ccin0", key=-77.2)
    b.op("gpsimd", lambda g: g.collective_compute(
            "AllGather", ALU.bypass, replica_groups=[list(range(NCORES))],
            ins=[cc_in.opt()], outs=[cc_out.opt()]),
         waits=["g_ccin0"], inc="sCC", n=1, label="g_ccw", key=-77)
    b.op("gpsimd", lambda g: g.dma_start(out=s_row2b[1:2, :], in_=bbr),
         inc="sWffn", n=16, label="w_r2b", key=-76)
    b.op("gpsimd", lambda g: g.dma_start(out=s_row2s[1:2, :], in_=b118),
         inc="sWffn", n=16, label="w_r2s", key=-75)

    IIA = 20.0
    for t in range(NT):
        p = t % NB
        rs = slice(t * 128, (t + 1) * 128)
        ka = lambda s, t=t: t * IIA + s

        b.op("sync", lambda s, p=p, rs=rs: s.dma_start(out=a_x[p][:], in_=x_in[rs, :]),
             waits=[L(f"s_u1n{t-NB}"), L(f"v_bn1_{t-NB}")],
             inc=f"sA_x{p}", n=16, label=f"ax{t}", key=ka(-25.0))

        b.op("vector", lambda v, p=p: v.bn_stats(a_st6[p][:, 0:6], a_x[p][:, 0:512]),
             waits=[f"ax{t}", L(f"v_agg1_{t-NB}")], inc="sV", key=ka(-15.0))
        b.op("vector", lambda v, p=p: v.bn_stats(a_st6[p][:, 6:12], a_x[p][:, 512:1024]),
             waits=[], inc="sV", label=f"v_bn1_{t}", key=ka(-14.9))
        b.op("vector", lambda v, p=p: v.bn_aggr(a_st2[p][:], a_st6[p][:]),
             waits=[f"v_bn1_{t}", L(f"s_ln1_{t-NB}")], inc="sV", label=f"v_agg1_{t}", key=ka(-14.0))
        b.op("scalar", lambda s, p=p: s.activation(a_lnv[p][:], a_st2[p][:, 1:2], AF.Ln, bias=s_eps[:]),
             waits=[f"v_agg1_{t}", L(f"s_rstd1_{t-NB}"), "v_eps"],
             inc="sS", label=f"s_ln1_{t}", key=ka(-11.4))
        b.op("scalar", lambda s, p=p: s.activation(a_rstd[p][:], a_lnv[p][:], AF.Exp, scale=-0.5),
             waits=[f"s_ln1_{t}", L(f"v_nmr1_{t-NB}"), L(f"s_u1n{t-NB}")],
             inc="sS", label=f"s_rstd1_{t}", key=ka(-11.2))
        b.op("vector", lambda v, p=p: v.tensor_scalar(
                a_nmr[p][:], a_st2[p][:, 0:1], a_rstd[p][:], -1.0, ALU.mult, ALU.mult),
             waits=[f"s_rstd1_{t}", L(f"s_u1n{t-NB}")], inc="sV", label=f"v_nmr1_{t}", key=ka(-7.5))
        b.op("scalar", lambda s, p=p: s.activation(
                a_u1[p][:], a_x[p][:], AF.Identity, bias=a_nmr[p][:], scale=a_rstd[p][:]),
             waits=[f"v_nmr1_{t}", L(f"t_Tu1_{t-NB}")], inc="sS", label=f"s_u1n{t}", key=ka(-7.2))

        def tu1(te, p=p):
            for c in range(C):
                ins = te.transpose(pT6[:, c * 128:(c + 1) * 128],
                                   a_u1[p][:, c * 128:(c + 1) * 128], s_id[:])
            return ins
        b.op("tensor", tu1, waits=[f"s_u1n{t}", L(f"s_u1fm_{t-1}"), "w_id"],
             inc="sT", label=f"t_Tu1_{t}", key=ka(7.0))
        b.op("scalar", lambda s, p=p: s.copy(a_u1fm[p][:], pT6[:]),
             waits=[f"t_Tu1_{t}", L(f"t_v_{t-NB}")], inc="sS", label=f"s_u1fm_{t}", key=ka(8.5))

        def proj(te, pdst, wsb, p=p):
            for h in range(2):
                cs = slice(h * 512, (h + 1) * 512)
                for c in range(C):
                    ins = te.matmul(pdst[:, cs], a_u1fm[p][:, c * 128:(c + 1) * 128],
                                    wsb[:, c, cs], start=(c == 0), stop=(c == C - 1),
                                    skip_group_check=True)
            return ins
        b.op("tensor", lambda te, p=p: proj(te, pQ[:], s_wq, p),
             waits=[f"s_u1fm_{t}", L(f"s_qev_{t-1}"), "w_wq"],
             inc="sT", label=f"t_q_{t}", key=ka(10.0))
        b.op("tensor", lambda te, p=p: proj(te, pK[:], s_wk, p),
             waits=[L(f"s_ekev_{t-1}"), "w_wk"], inc="sT", label=f"t_k_{t}", key=ka(11.5))
        b.op("tensor", lambda te, p=p: proj(te, pV[:], s_wv, p),
             waits=[L(f"s_eqev_{t-1}"), "w_wv"], inc="sT", label=f"t_v_{t}", key=ka(13.5))

        b.op("vector", lambda v, p=p: v.tensor_add(a_qsb[p][:], pQ[:], s_bqb[:]),
             waits=[f"t_q_{t}", "w_bqb", L(f"v_ssq_{t-NB}"), L(f"t_Tq_{t-NB}")],
             inc="sV", label=f"s_qev_{t}", key=ka(12.0))
        b.op("vector", lambda v, p=p: v.tensor_add(a_ksb[p][:], pK[:], s_bkb[:]),
             waits=[f"t_k_{t}", "w_bkb", L(f"v_ssk_{t-NB}"), L(f"t_Tk_{t-NB}")],
             inc="sV", label=f"s_kev_{t}", key=ka(13.5))
        b.op("vector", lambda v, p=p: v.tensor_add(a_v[p][:], pV[:], s_bvb[:]),
             waits=[f"t_v_{t}", "w_bvb", L(f"g_v_{t-NB}")], inc="sV", label=f"v_vev_{t}", key=ka(15.0))

        b.op("vector", lambda v, p=p, t=t: v.scalar_tensor_tensor(
                s_dump[:], a_qsb[p][:], 1.0, a_qsb[p][:], ALU.bypass, ALU.mult,
                accum_out=s_accq[:, t:t + 1]),
             waits=[f"s_qev_{t}", f"v_vev_{t}"], inc="sV", label=f"v_ssq_{t}", key=ka(15.5))
        b.op("vector", lambda v, p=p, t=t: v.scalar_tensor_tensor(
                s_dump[:], a_ksb[p][:], 1.0, a_ksb[p][:], ALU.bypass, ALU.mult,
                accum_out=s_acck[:, t:t + 1]),
             waits=[f"s_kev_{t}", f"v_ssq_{t}"], inc="sV", label=f"v_ssk_{t}", key=ka(15.8))

        def tq(te, p=p):
            for c in range(C):
                ins = te.transpose(pT7[:, c * 128:(c + 1) * 128],
                                   a_qsb[p][:, c * 128:(c + 1) * 128], s_id[:])
            return ins
        b.op("tensor", tq, waits=[f"s_qev_{t}", L(f"s_kfm_{t-1}")],
             inc="sT", label=f"t_Tq_{t}", key=ka(13.0))
        def tk(te, p=p):
            for c in range(C):
                ins = te.transpose(pT7[:, c * 128:(c + 1) * 128],
                                   a_ksb[p][:, c * 128:(c + 1) * 128], s_id[:])
            return ins
        b.op("tensor", tk, waits=[f"s_kev_{t}", f"s_qfm_{t}"],
             inc="sT", label=f"t_Tk_{t}", key=ka(14.2))
        b.op("scalar", lambda s, p=p: s.copy(a_qfm[p][:], pT7[:]),
             waits=[f"t_Tq_{t}", L(f"t_eq_{t-NB}")], inc="sS", label=f"s_qfm_{t}", key=ka(13.4))
        b.op("scalar", lambda s, p=p: s.copy(a_kfm[p][:], pT7[:]),
             waits=[f"t_Tk_{t}", L(f"t_ek_{t-NB}")], inc="sS", label=f"s_kfm_{t}", key=ka(14.6))

        def eqmm(te, src, pdst, p=p):
            for c in range(C):
                ins = te.matmul(pdst[:, c * 128:(c + 1) * 128],
                                src[p][:, c * 128:(c + 1) * 128],
                                s_wbd[:, c, :], start=(c % 4 == 0),
                                stop=(c % 4 == 3), skip_group_check=True)
            return ins
        b.op("tensor", lambda te, p=p: eqmm(te, a_qfm, pV[:], p),
             waits=[f"s_qfm_{t}", f"v_vev_{t}", "w_a"], inc="sT", label=f"t_eq_{t}", key=ka(27.3))
        b.op("tensor", lambda te, p=p: eqmm(te, a_kfm, pK[:], p),
             waits=[f"s_kfm_{t}", f"s_kev_{t}"], inc="sT", label=f"t_ek_{t}", key=ka(27.7))
        b.op("scalar", lambda s, p=p: s.copy(a_eq[p][:], pV[:]),
             waits=[f"t_eq_{t}", L(f"g_v_{t-NB}")], inc="sS", label=f"s_eqev_{t}", key=ka(28.6))
        b.op("scalar", lambda s, p=p: s.copy(a_ek[p][:], pK[:]),
             waits=[f"t_ek_{t}", L(f"g_v_{t-NB}")], inc="sS", label=f"s_ekev_{t}", key=ka(28.9))

        b.op("gpsimd", lambda g, p=p, rs=rs: g.dma_start(out=eq_sp[rs, :], in_=a_eq[p][:]),
             waits=[f"s_eqev_{t}"], inc=f"sA_sp{p}", n=16, label=f"g_eq_{t}", key=ka(29.45))
        b.op("gpsimd", lambda g, p=p, rs=rs: g.dma_start(out=ek_sp[rs, :], in_=a_ek[p][:]),
             waits=[f"s_ekev_{t}"], inc=f"sA_sp{p}", n=16, label=f"g_ek_{t}", key=ka(29.55))
        b.op("gpsimd", lambda g, p=p, rs=rs: g.dma_start(out=v_sp[rs, :], in_=a_v[p][:]),
             waits=[f"v_vev_{t}"], inc=f"sA_sp{p}", n=16, label=f"g_v_{t}", key=ka(29.8))

    TL = NT - 1
    FZ = NT * IIA + 25.0
    b.op("vector", lambda v: v.reduce_sum(s_nrm[:, 0:1], s_accq[:], axis=AX.X),
         waits=[f"v_ssq_{TL}"], inc="sV", label="vf_q", key=FZ + 0)
    b.op("vector", lambda v: v.reduce_sum(s_nrm[:, 1:2], s_acck[:], axis=AX.X),
         waits=[f"v_ssk_{TL}"], inc="sV", label="vf_k", key=FZ + 1)
    b.op("sync", lambda g: g.dma_start(out=cc_in, in_=s_nrm[:]),
         waits=["vf_k", "g_ccw"], inc="sCCd", n=16, label="g_ccin", key=FZ + 2)
    b.op("gpsimd", lambda g: g.collective_compute(
            "AllGather", ALU.bypass, replica_groups=[list(range(NCORES))],
            ins=[cc_in.opt()], outs=[cc_out.opt()]),
         waits=["g_ccin"], inc="sCC", n=1, label="g_cc", key=FZ + 3)
    for r in range(NCORES):
        b.op("gpsimd", lambda g, r=r: g.dma_start(
                out=s_ag8[:, r, :], in_=cc_out[r * 128:(r + 1) * 128, :]),
             waits=["g_cc"], inc="sCCg", n=16,
             label=f"g_ag{r}", key=FZ + 4 + r * 0.01)
    b.op("vector", lambda v: v.reduce_sum(
            s_S128[:], s_ag8[:].rearrange("p r c -> p c r"), axis=AX.X),
         waits=[f"g_ag{NCORES-1}"], inc="sV", label="v_S128", key=FZ + 5)
    b.op("scalar", lambda s: s.activation(s_lnS[:], s_S128[:], AF.Ln),
         waits=["v_S128"], inc="sS", label="s_lnS", key=FZ + 14)
    b.op("scalar", lambda s: s.activation(s_rn[:], s_lnS[:], AF.Exp, scale=-0.5),
         waits=["s_lnS"], inc="sS", label="s_rn", key=FZ + 15)
    b.op("vector", lambda v: v.tensor_mul(s_t2[:], s_S128[:], s_rn[:]),
         waits=["s_rn"], inc="sV", label="v_t2a", key=FZ + 16)
    b.op("vector", lambda v: v.tensor_mul(s_t2[:], s_t2[:], s_rn[:]),
         waits=["v_t2a"], inc="sV", label="v_t2b", key=FZ + 17)
    b.op("vector", lambda v: v.reduce_sum(s_ts[:], s_t2[:], axis=AX.X),
         waits=["v_t2b"], inc="sV", label="v_ts", key=FZ + 18)
    b.op("scalar", lambda s: s.activation(s_scq[:], s_ts[:], AF.Exp, scale=-0.5),
         waits=["v_ts"], inc="sS", label="s_scq", key=FZ + 19)
    b.op("vector", lambda v: v.tensor_scalar(s_scqk[:], s_scq[:], 1.0 / 64.0, None, ALU.mult),
         waits=["s_scq"], inc="sV", label="v_scqk", key=FZ + 20)
    b.op("scalar", lambda s: s.activation(s_lnscqk[:], s_scqk[:], AF.Ln),
         waits=["v_scqk"], inc="sS", label="s_lnscqk", key=FZ + 20.5)

    def csb(te):
        for h in range(2):
            cs = slice(h * 512, (h + 1) * 512)
            for c in range(C):
                ins = te.matmul(pQ[0:1, cs], s_onebc[:], s_wb[:, c, cs],
                                start=(c == 0), stop=(c == C - 1),
                                skip_group_check=True)
        return ins
    b.op("tensor", csb, waits=[f"s_qev_{TL}", "w_r2s", "v_onebc"],
         inc="sT", label="t_csb", key=FZ + 22)
    def css(te):
        for c in range(C):
            ins = te.matmul(pK[0:1, 0:SMALL], s_onebc[:], s_ws[:, c, :],
                            start=(c == 0), stop=(c == C - 1),
                            skip_group_check=True)
        return ins
    b.op("tensor", css, waits=[f"s_ekev_{TL}", "w_r2s"],
         inc="sT", label="t_css", key=FZ + 22.2)
    b.op("scalar", lambda s: s.copy(s_row2b[0:1, :], pQ[0:1, :]),
         waits=["t_csb"], inc="sS", label="s_csbev", key=FZ + 22.5)
    b.op("scalar", lambda s: s.copy(s_row2s[0:1, :], pK[0:1, 0:SMALL]),
         waits=["t_css"], inc="sS", label="s_cssev", key=FZ + 23.5)

    adone = ["vf_k", f"s_ekev_{TL}", f"t_ek_{TL}"] + \
            [L(f"g_v_{TL - i}") for i in range(min(NB, NT))]
    for i in range(NBB):
        b.op("vector", lambda v, i=i: v.memset(b_g6[i][:], -1e30),
             waits=adone, inc="sV", label=f"v_mg{i}", key=FZ + 21 + i * 0.1)

    IIB = 11.0
    BZ = FZ + 30.0
    for t in range(NT):
        p = t % NBB
        pp = t & 1
        kb = lambda s, t=t: BZ + t * IIB + s
        rs = slice(t * 128, (t + 1) * 128)
        pF = pQ if pp == 0 else pK
        pSm = pV[:, 0:SMALL] if pp == 0 else pV[:, 512:512 + SMALL]
        pT = pT6 if pp == 0 else pT7
        first = adone if t < NBB else []

        b.op("sync", lambda s, p=p, rs=rs: s.dma_start(out=b_x[p][:], in_=x_in[rs, :]),
             waits=[L(f"g_out_{t-NBB}")] + first, inc=f"sB_in{p}", n=16, label=f"b_lx_{t}", key=kb(6.4))
        b.op("sync", lambda s, p=p, rs=rs: s.dma_start(out=b_eq[p][:], in_=eq_sp[rs, :]),
             waits=[f"g_v_{t}", L(f"v_sraw_{t-NBB}")] + first, inc=f"sB_in{p}", n=16, label=f"b_leq_{t}", key=kb(6.5))
        b.op("sync", lambda s, p=p, rs=rs: s.dma_start(out=b_ek[p][:], in_=ek_sp[rs, :]),
             waits=[L(f"t_Tu2_{t-NBB}")] + first, inc=f"sB_in{p}", n=16, label=f"b_lek_{t}", key=kb(6.6))
        b.op("gpsimd", lambda g, p=p, rs=rs: g.dma_start(out=b_v[p][:], in_=v_sp[rs, :]),
             waits=[f"g_v_{t}", L(f"po_x2_{t-NBB}")] + first, inc=f"sB_v{p}", n=16,
             label=f"b_lv_{t}", key=kb(6.7) if t < NBB else kb(-13.5))

        e12 = b_z[p].bitcast(BF16)
        b.op("scalar", lambda s, p=p, e12=e12: s.activation(
                e12[:, 0:D], b_eq[p][:], AF.Exp, scale=s_rn[:, 0:1],
                bias=s_lnscqk[:]),
             waits=[f"b_lek_{t}", "s_rn", "s_lnscqk", L(f"po_x1_{t-NBB}")],
             inc="sS", label=f"s_e1_{t}", key=kb(8.0))
        b.op("scalar", lambda s, p=p, e12=e12: s.activation(
                e12[:, D:2 * D], b_ek[p][:], AF.Exp, scale=s_rn[:, 1:2]),
             waits=[], inc="sS", label=f"s_e2_{t}", key=kb(8.3))
        b.op("vector", lambda v, p=p, e12=e12: v.tensor_mul(
                b_e[p][:], e12[:, 0:D], e12[:, D:2 * D]),
             waits=[f"s_e2_{t}", L(f"v_sraw_{t-NBB}")], inc="sV", label=f"s_e_{t}", key=kb(15.0))
        b.op("vector", lambda v, p=p: v.reduce_sum(
                b_s[p][:], b_e[p][:].rearrange("t (h m) -> t h m", h=H), axis=AX.X),
             waits=[f"s_e_{t}", L(f"v_w_{t-NBB}")], inc="sV", label=f"v_sraw_{t}", key=kb(16.5))
        b.op("vector", lambda v, p=p: v.reciprocal(b_r[p][:], b_s[p][:]),
             waits=[f"v_sraw_{t}"], inc="sV", label=f"v_recip_{t}", key=kb(17.2))
        b.op("vector", lambda v, p=p: v.tensor_mul(b_w[p][:], b_s[p][:], b_r[p][:]),
             waits=[f"v_recip_{t}"], inc="sV", label=f"v_w_{t}", key=kb(17.4))
        b.op("vector", lambda v, p=p: v.tensor_tensor(
                b_ao[p][:, 0:512].rearrange("t (h m) -> t h m", h=H // 2),
                b_v[p][:, 0:512].rearrange("t (h m) -> t h m", h=H // 2),
                b_w[p][:, 0:8].unsqueeze(2).broadcast_to([128, H // 2, HD]), ALU.mult),
             waits=[f"b_lv_{t}", f"v_w_{t}", L(f"s_ln2o_{t-NBB}"), L(f"v_bn2_{t-NBB}")],
             inc="sV", label=f"po_aoa_{t}", key=kb(18.0))
        b.op("vector", lambda v, p=p: v.tensor_tensor(
                b_ao[p][:, 512:1024].rearrange("t (h m) -> t h m", h=H // 2),
                b_v[p][:, 512:1024].rearrange("t (h m) -> t h m", h=H // 2),
                b_w[p][:, 8:16].unsqueeze(2).broadcast_to([128, H // 2, HD]), ALU.mult),
             waits=[], inc="sV", label=f"po_ao_{t}", key=kb(18.3))

        b.op("vector", lambda v, p=p: v.bn_stats(b_st6[p][:, 0:6], b_ao[p][:, 0:512]),
             waits=[f"po_aoa_{t}", L(f"v_agg4_{t-NBB}")], inc="sV", key=kb(19.0))
        b.op("vector", lambda v, p=p: v.bn_stats(b_st6[p][:, 6:12], b_ao[p][:, 512:1024]),
             waits=[f"po_ao_{t}"], inc="sV", label=f"v_bn2_{t}", key=kb(19.1))
        b.op("vector", lambda v, p=p: v.bn_aggr(b_st2[p][:], b_st6[p][:]),
             waits=[f"v_bn2_{t}", L(f"s_ln4_{t-NBB}")], inc="sV", label=f"v_agg2_{t}", key=kb(19.5))
        b.op("scalar", lambda s, p=p: s.activation(b_lnv[p][:], b_st2[p][:, 1:2], AF.Ln, bias=s_eps[:]),
             waits=[f"v_agg2_{t}", L(f"s_rstd4_{t-NBB}"), "v_eps"],
             inc="sS", label=f"s_ln2_{t}", key=kb(20.0))
        b.op("scalar", lambda s, p=p: s.activation(b_rstd[p][:], b_lnv[p][:], AF.Exp, scale=-0.5),
             waits=[f"s_ln2_{t}", L(f"v_nmr4_{t-NBB}"), L(f"s_ln4o_{t-NBB}")],
             inc="sS", label=f"s_rstd2_{t}", key=kb(20.3))
        b.op("vector", lambda v, p=p: v.tensor_scalar(
                b_nmr[p][:], b_st2[p][:, 0:1], b_rstd[p][:], -1.0, ALU.mult, ALU.mult),
             waits=[f"s_rstd2_{t}", L(f"s_ln4o_{t-NBB}")], inc="sV", label=f"v_nmr2_{t}", key=kb(20.6))
        b.op("scalar", lambda s, p=p: s.activation(
                b_z[p][:, 0:512], b_ao[p][:, 0:512], AF.Identity,
                bias=b_nmr[p][:], scale=b_rstd[p][:]),
             waits=[f"v_nmr2_{t}", f"s_e_{t}", f"s_e2_{t}"], inc="sS",
             label=f"s_ln2oa_{t}", key=kb(21.0))
        b.op("scalar", lambda s, p=p: s.activation(
                b_z[p][:, 512:1024], b_ao[p][:, 512:1024], AF.Identity,
                bias=b_nmr[p][:], scale=b_rstd[p][:]),
             waits=[], inc="sS", label=f"s_ln2o_{t}", key=kb(21.3))
        b.op("gpsimd", lambda g, p=p: g.tensor_add(
                b_x1[p][:, 0:512], b_x[p][:, 0:512], b_z[p][:, 0:512]),
             waits=[f"s_ln2oa_{t}", f"b_lek_{t}", L(f"po_x2_{t-NBB}"),
                    L(f"s_x1c_{t-NBB}"), L(f"v_bn3_{t-NBB}")],
             inc="sP", label=f"po_x1a_{t}", key=kb(22.5))
        b.op("gpsimd", lambda g, p=p: g.tensor_add(
                b_x1[p][:, 512:1024], b_x[p][:, 512:1024], b_z[p][:, 512:1024]),
             waits=[f"s_ln2o_{t}"], inc="sP", label=f"po_x1_{t}", key=kb(22.8))

        b.op("vector", lambda v, p=p: v.bn_stats(b_st6[p][:, 0:6], b_x1[p][:, 0:512]),
             waits=[f"po_x1a_{t}", f"v_agg2_{t}"], inc="sV", key=kb(24.0))
        b.op("vector", lambda v, p=p: v.bn_stats(b_st6[p][:, 6:12], b_x1[p][:, 512:1024]),
             waits=[f"po_x1_{t}"], inc="sV", label=f"v_bn3_{t}", key=kb(24.1))
        b.op("vector", lambda v, p=p: v.bn_aggr(b_st2[p][:], b_st6[p][:]),
             waits=[f"v_bn3_{t}", f"s_ln2_{t}"], inc="sV", label=f"v_agg3_{t}", key=kb(24.5))
        b.op("scalar", lambda s, p=p: s.activation(b_lnv[p][:], b_st2[p][:, 1:2], AF.Ln, bias=s_eps[:]),
             waits=[f"v_agg3_{t}", f"s_rstd2_{t}"], inc="sS", label=f"s_ln3_{t}", key=kb(25.0))
        b.op("scalar", lambda s, p=p: s.activation(b_r3[p][:], b_lnv[p][:], AF.Exp, scale=-0.5),
             waits=[f"s_ln3_{t}", L(f"v_ngs_{t-NBB}"), L(f"s_exg_{t-NBB}")],
             inc="sS", label=f"s_rstd3_{t}", key=kb(25.3))
        b.op("scalar", lambda s, p=p: s.activation(
                b_r2[p][:, 0:1], b_st2[p][:, 0:1], AF.Identity, scale=-1.0),
             waits=[f"s_rstd3_{t}", L(f"t_Trow_{t-NBB}")], inc="sS",
             label=f"s_nmu3_{t}", key=kb(25.5))
        b.op("scalar", lambda s, p=p: s.activation(
                b_r2[p][:, 1:2], b_lnv[p][:], AF.Exp, scale=0.5),
             waits=[f"s_nmu3_{t}"], inc="sS", label=f"s_sg3_{t}", key=kb(25.65))
        b.op("scalar", lambda s, p=p: s.activation(
                b_r3sq[p][:], b_lnv[p][:], AF.Exp, scale=-1.0),
             waits=[f"s_sg3_{t}", L(f"s_ln4_{t-NBB}")], inc="sS",
             label=f"v_r3sq_{t}", key=kb(25.7))
        b.op("scalar", lambda s, p=p: s.activation(
                b_lnr3[p][:], b_lnv[p][:], AF.Identity, scale=-0.5),
             waits=[f"v_r3sq_{t}", L(f"s_rstd4_{t-NBB}")], inc="sS",
             label=f"v_lnr3_{t}", key=kb(25.75))
        b.op("scalar", lambda s, p=p: s.copy(b_u2[p][:, 0:512], b_x1[p][:, 0:512]),
             waits=[f"po_x1a_{t}", f"s_e2_{t}", f"s_e_{t}"], inc="sS",
             label=f"s_x1ca_{t}", key=kb(23.1))
        b.op("scalar", lambda s, p=p: s.copy(b_u2[p][:, 512:1024], b_x1[p][:, 512:1024]),
             waits=[f"po_x1_{t}"], inc="sS",
             label=f"s_x1c_{t}", key=kb(23.3))

        def tu2(te, lo, hi, p=p, pT=pT):
            for c in range(lo, hi):
                ins = te.transpose(pT[:, c * 128:(c + 1) * 128],
                                   b_u2[p][:, c * 128:(c + 1) * 128], s_id[:])
            return ins
        tb1waits = [f"s_x1ca_{t}", L(f"s_c2fm_{t-2}"), L(f"s_rowev_{t-1}")]
        if t == 0:
            tb1waits.append(f"s_u1fm_{TL}")
        if t == 1:
            tb1waits.append(f"s_kfm_{TL}")
        b.op("tensor", lambda te, f=tu2: f(te, 0, 4), waits=tb1waits,
             inc="sT", label=f"t_Tu2a_{t}", key=kb(27.4))
        b.op("tensor", lambda te, f=tu2: f(te, 4, 8), waits=[f"s_x1c_{t}"],
             inc="sT", label=f"t_Tu2_{t}", key=kb(27.5))
        b.op("vector", lambda v, p=p, pT=pT: v.tensor_scalar(
                b_u2fm[p][:], pT[:], 0.0, None, ALU.add),
             waits=[f"t_Tu2_{t}", L(f"t_base_{t-NBB}")], inc="sV", label=f"s_u2fm_{t}", key=kb(28.5))

        def small1(te, p=p, pSm=pSm):
            for c in range(C):
                ins = te.matmul(pSm, b_u2fm[p][:, c * 128:(c + 1) * 128],
                                s_ws[:, c, :], start=(c == 0), stop=False,
                                skip_group_check=True)
            return ins
        s1waits = [f"s_u2fm_{t}", L(f"s_cat1_{t-2}"), "w_r2s"]
        if t <= 1:
            s1waits.append(f"s_eqev_{TL}")
        b.op("tensor", small1, waits=s1waits, inc="sT", label=f"t_s1_{t}", key=kb(29.5))

        pTo = pT7 if pp == 0 else pT6
        trwaits = [f"s_sg3_{t}", L(f"s_c2fm_{t-1}")]
        if t == 0:
            trwaits.append(f"s_kfm_{TL}")
        b.op("tensor", lambda te, p=p, pTo=pTo: te.transpose(
                pTo[0:2, 0:128], b_r2[p][:], s_id[:]),
             waits=trwaits, inc="sT", label=f"t_Trow_{t}", key=kb(29.6))
        b.op("scalar", lambda s, p=p, pTo=pTo: s.copy(b_r2T[p][:], pTo[0:2, 0:128]),
             waits=[f"t_Trow_{t}", L(f"t_fix_{t-NBB}")], inc="sS",
             label=f"s_rowev_{t}", key=kb(29.7))
        b.op("tensor", lambda te, p=p, pSm=pSm: te.matmul(
                pSm, b_r2T[p][:], s_row2s[:], start=False, stop=True,
                skip_group_check=True),
             waits=[f"s_rowev_{t}", "s_cssev"], inc="sT", label=f"t_s1f_{t}", key=kb(29.8))

        def base(te, p=p, pF=pF):
            for h in range(2):
                cs = slice(h * 512, (h + 1) * 512)
                for c in range(C):
                    ins = te.matmul(pF[:, cs], b_u2fm[p][:, c * 128:(c + 1) * 128],
                                    s_wb[:, c, cs], start=(c == 0), stop=False,
                                    skip_group_check=True)
            return ins
        bwaits = [f"s_u2fm_{t}", L(f"s_ln4o_{t-2}"), "w_r2s"]
        if t == 0:
            bwaits.append("s_csbev")
        if t == 1:
            bwaits.append("s_cssev")
        b.op("tensor", base, waits=bwaits, inc="sT", label=f"t_base_{t}", key=kb(30.0))
        def fix(te, p=p, pF=pF):
            for h in range(2):
                cs = slice(h * 512, (h + 1) * 512)
                ins = te.matmul(pF[:, cs], b_r2T[p][:], s_row2b[:, cs],
                                start=False, stop=False, skip_group_check=True)
            return ins
        b.op("tensor", fix, waits=[f"s_rowev_{t}", "s_csbev"],
             inc="sT", label=f"t_fix_{t}", key=kb(30.2))

        b.op("scalar", lambda s, p=p, pSm=pSm: s.copy(b_cat1[p][:], pSm),
             waits=[f"t_s1f_{t}", L(f"v_shc_{t-NBB}")], inc="sS", label=f"s_cat1_{t}", key=kb(30.5))
        b.op("vector", lambda v, p=p: v.reduce_max(
                b_ngm[p][:], b_cat1[p][:, 112:118], axis=AX.X, negate=True),
             waits=[f"s_cat1_{t}", L(f"s_exg_{t-NBB}")], inc="sV", label=f"v_ngm_{t}", key=kb(31.0))
        b.op("vector", lambda v, p=p: v.tensor_scalar(
                b_ngm[p][:], b_ngm[p][:], b_r3[p][:], None, ALU.mult),
             waits=[f"v_ngm_{t}", f"s_rstd3_{t}"], inc="sV", label=f"v_ngs_{t}", key=kb(31.1))
        b.op("scalar", lambda s, p=p: s.activation(
                b_g6[p][:, 0:6], b_cat1[p][:, 112:118], AF.Exp, bias=b_ngm[p][:],
                scale=b_r3[p][:], accum_out=b_tot[p][:]),
             waits=[f"v_ngs_{t}", L(f"v_gate_{t-NBB}"), L(f"v_top8_{t-NBB}"),
                    L(f"v_rt_{t-NBB}"), f"v_mg{p}"],
             inc="sS", label=f"s_exg_{t}", key=kb(31.3))
        b.op("vector", lambda v, p=p: v.reciprocal(b_ngm[p][:], b_tot[p][:]),
             waits=[f"s_exg_{t}"], inc="sV", label=f"v_grec_{t}", key=kb(31.8))
        b.op("vector", lambda v, p=p: v.tensor_scalar(
                b_g6[p][:, 0:6], b_g6[p][:, 0:6], b_ngm[p][:], None, ALU.mult),
             waits=[f"v_grec_{t}"], inc="sV", label=f"v_gate_{t}", key=kb(32.0))
        b.op("vector", lambda v, p=p: v.max(b_t8[p][:], b_g6[p][:]),
             waits=[f"v_gate_{t}", L(f"v_swt_{t-NBB}")], inc="sV", label=f"v_top8_{t}", key=kb(32.2))
        b.op("vector", lambda v, p=p: v.scalar_tensor_tensor(
                b_sw[p][:, 0:6], b_g6[p][:, 0:6], b_t8[p][:, 1:2],
                b_g6[p][:, 0:6], ALU.is_ge, ALU.mult, accum_out=b_tot[p][:]),
             waits=[f"v_top8_{t}", L(f"v_spw_{t-NBB}")], inc="sV", label=f"v_swt_{t}", key=kb(32.4))
        b.op("vector", lambda v, p=p: v.tensor_scalar(
                b_tot[p][:], b_tot[p][:], 1e-6, None, ALU.add),
             waits=[f"v_swt_{t}"], inc="sV", label=f"v_tot1_{t}", key=kb(33.0))
        b.op("vector", lambda v, p=p: v.reciprocal(b_ngm[p][:], b_tot[p][:]),
             waits=[f"v_tot1_{t}", f"v_gate_{t}"], inc="sV", label=f"v_rt_{t}", key=kb(33.2))
        b.op("vector", lambda v, p=p: v.tensor_scalar(
                b_sw[p][:, 0:6], b_sw[p][:, 0:6], b_ngm[p][:], None, ALU.mult),
             waits=[f"v_rt_{t}"], inc="sV", label=f"v_spw_{t}", key=kb(33.4))
        b.op("gpsimd", lambda g, p=p: g.tensor_tensor(
                b_cat2[p][:, 0:96].rearrange("t (e r) -> t e r", e=E),
                b_cat1[p][:, 0:96].rearrange("t (e r) -> t e r", e=E),
                b_sw[p][:, 0:6].unsqueeze(2).broadcast_to([128, E, R]), ALU.mult),
             waits=[f"v_spw_{t}", L(f"t_Tc2_{t-NBB}")], inc="sP", label=f"v_hsc_{t}", key=kb(33.6))
        b.op("scalar", lambda s, p=p: s.copy(
                b_cat2[p][:, 96:112], b_cat1[p][:, 96:112]),
             waits=[f"v_hsc_{t}", f"s_cat1_{t}"], inc="sS", label=f"v_shc_{t}", key=kb(33.8))

        b.op("tensor", lambda te, p=p, pT=pT: te.transpose(
                pT[0:CAT2, 0:128], b_cat2[p][:], s_id[:]),
             waits=[f"v_shc_{t}", f"s_u2fm_{t}"], inc="sT", label=f"t_Tc2_{t}", key=kb(34.5))
        b.op("vector", lambda v, p=p, pT=pT: v.tensor_scalar(
                b_c2fm[p][:], pT[0:CAT2, 0:128], 0.0, None, ALU.add),
             waits=[f"t_Tc2_{t}", L(f"t_ffn_{t-NBB}")], inc="sV", label=f"s_c2fm_{t}", key=kb(35.0))
        def mm2(te, p=p, pF=pF):
            for h in range(2):
                cs = slice(h * 512, (h + 1) * 512)
                ins = te.matmul(pF[:, cs], b_c2fm[p][:], s_w2[:, cs],
                                start=False, stop=(h == 1), skip_group_check=True)
            return ins
        b.op("tensor", mm2, waits=[f"s_c2fm_{t}", f"t_base_{t}", "w_r2s"],
             inc="sT", label=f"t_ffn_{t}", key=kb(35.5))
        b.op("vector", lambda v, p=p, pF=pF: v.bn_stats(b_st6[p][:, 0:6], pF[:, 0:512]),
             waits=[f"t_ffn_{t}", f"v_agg3_{t}"], inc="sV", key=kb(37.5))
        b.op("vector", lambda v, p=p, pF=pF: v.bn_stats(b_st6[p][:, 6:12], pF[:, 512:1024]),
             waits=[], inc="sV", label=f"v_bn4_{t}", key=kb(37.6))
        b.op("vector", lambda v, p=p: v.bn_aggr(b_st2[p][:], b_st6[p][:]),
             waits=[f"v_bn4_{t}", f"s_ln3_{t}"], inc="sV", label=f"v_agg4_{t}", key=kb(38.0))
        b.op("scalar", lambda s, p=p: s.activation(
                b_lnv[p][:], b_st2[p][:, 1:2], AF.Ln, bias=s_eps[:],
                scale=b_r3sq[p][:]),
             waits=[f"v_agg4_{t}", f"v_r3sq_{t}", f"s_sg3_{t}", f"v_lnr3_{t}"],
             inc="sS", label=f"s_ln4_{t}", key=kb(38.5))
        b.op("scalar", lambda s, p=p: s.activation(
                b_rstd[p][:], b_lnv[p][:], AF.Exp, scale=-0.5, bias=b_lnr3[p][:]),
             waits=[f"s_ln4_{t}", f"v_nmr2_{t}", f"s_ln2o_{t}"],
             inc="sS", label=f"s_rstd4_{t}", key=kb(38.8))
        b.op("vector", lambda v, p=p: v.tensor_scalar(
                b_nmr[p][:], b_st2[p][:, 0:1], b_rstd[p][:], -1.0, ALU.mult, ALU.mult),
             waits=[f"s_rstd4_{t}", f"s_ln2o_{t}"], inc="sV", label=f"v_nmr4_{t}", key=kb(39.1))
        b.op("scalar", lambda s, p=p, pF=pF: s.activation(
                b_z[p][:, 0:512], pF[:, 0:512], AF.Identity,
                bias=b_nmr[p][:], scale=b_rstd[p][:]),
             waits=[f"v_nmr4_{t}", f"po_x1_{t}"], inc="sS", label=f"s_ln4oa_{t}", key=kb(39.5))
        b.op("scalar", lambda s, p=p, pF=pF: s.activation(
                b_z[p][:, 512:1024], pF[:, 512:1024], AF.Identity,
                bias=b_nmr[p][:], scale=b_rstd[p][:]),
             waits=[], inc="sS", label=f"s_ln4o_{t}", key=kb(39.8))
        b.op("gpsimd", lambda g, p=p: g.tensor_add(
                b_x[p][:, 0:512], b_x1[p][:, 0:512], b_z[p][:, 0:512]),
             waits=[f"s_ln4oa_{t}"], inc="sP", label=f"po_x2a_{t}", key=kb(40.5))
        b.op("gpsimd", lambda g, p=p: g.tensor_add(
                b_x[p][:, 512:1024], b_x1[p][:, 512:1024], b_z[p][:, 512:1024]),
             waits=[f"s_ln4o_{t}"], inc="sP", label=f"po_x2_{t}", key=kb(41.0))
        b.op("gpsimd", lambda g, p=p, rs=rs: g.dma_start(out=y_out[rs, :], in_=b_x[p][:]),
             waits=[f"po_x2_{t}"], inc=f"sB_out{p}", n=16, label=f"g_out_{t}", key=kb(42.0))

    def nt_i(i):
        return len([t for t in range(NT) if t % NB == i])
    def nt_b(i):
        return len([t for t in range(NT) if t % NBB == i])
    retire = [("sCCd", 32), ("sCCg", NCORES * 16), ("sCC", 2),
              ("sWa", 16), ("sWid", 16), ("sWq", 16), ("sWk", 16), ("sWv", 16),
              ("sWqb", 16), ("sWkb", 16), ("sWvb", 16), ("sWffn", 80)]
    for i in range(NB):
        retire += [(f"sA_x{i}", nt_i(i) * 16), (f"sA_sp{i}", nt_i(i) * 48)]
    for i in range(NBB):
        retire += [(f"sB_out{i}", nt_b(i) * 16), (f"sB_in{i}", nt_b(i) * 48),
                   (f"sB_v{i}", nt_b(i) * 16)]
    b.op("gpsimd", lambda g: g.nop(), waits=retire, key=1e9)

    b.resolve()
    with nc.Block() as block:
        def mkbody(ename):
            def body(eng):
                last = {}
                for r in b.ops[ename]:
                    for semn, val in r["rw"]:
                        if last.get(semn, 0) < val:
                            eng.wait_ge(sems[semn], val)
                            last[semn] = val
                    ins = r["fn"](eng)
                    if r["inc"] is not None:
                        ins.then_inc(sems[r["inc"]], r["n"])
            return body
        block.sync(mkbody("sync"))
        block.gpsimd(mkbody("gpsimd"))
        block.vector(mkbody("vector"))
        block.scalar(mkbody("scalar"))
        block.tensor(mkbody("tensor"))

    es.close()
    return nc


def _host_layout(inputs):
    f32 = np.float32
    Wq = np.asarray(inputs["L0_attn_Wq"], f32)
    Wk = np.asarray(inputs["L0_attn_Wk"], f32)
    Wv = np.asarray(inputs["L0_attn_Wv"], f32)
    Wb = np.asarray(inputs["L0_ffn_base_W"], f32)
    weights = np.asarray(inputs["L0_attn_weights"], f32)
    shared_A = np.asarray(inputs["L0_ffn_shared_A"], f32)
    shared_B = np.asarray(inputs["L0_ffn_shared_B"], f32)
    routed_A = np.asarray(inputs["L0_ffn_routed_A"], f32)
    routed_B = np.asarray(inputs["L0_ffn_routed_B"], f32)
    router_W = np.asarray(inputs["L0_ffn_router_W"], f32)
    router_b = np.asarray(inputs["L0_ffn_router_b"], f32)

    wm = {}
    wm["wqT"] = np.ascontiguousarray(Wq.T)
    wm["wkT"] = np.ascontiguousarray(Wk.T)
    wm["wvT"] = np.ascontiguousarray(Wv.T)
    wm["wbT"] = np.ascontiguousarray(Wb.T)
    bd = np.zeros((C, 128, 128), f32)
    for c in range(C):
        bd[c, 0:64, 0:64] = weights[2 * c].T
        bd[c, 64:128, 64:128] = weights[2 * c + 1].T
    wm["wbd"] = bd
    wsmall = np.concatenate([routed_A.reshape(96, D), shared_A, router_W], 0)
    wm["wsT"] = np.ascontiguousarray(wsmall.T)
    wm["w2"] = np.ascontiguousarray(
        np.concatenate([routed_B.transpose(0, 2, 1).reshape(96, D), shared_B.T], 0))
    wm["bqr"] = np.ascontiguousarray(inputs["L0_attn_bq"].astype(f32)[None, :])
    wm["bkr"] = np.ascontiguousarray(inputs["L0_attn_bk"].astype(f32)[None, :])
    wm["bvr"] = np.ascontiguousarray(inputs["L0_attn_bv"].astype(f32)[None, :])
    wm["bbr"] = np.ascontiguousarray(inputs["L0_ffn_base_b"].astype(f32)[None, :])
    wm["b118"] = np.ascontiguousarray(
        np.concatenate([np.zeros(CAT2, f32), router_b])[None, :])
    wm["identd"] = np.eye(128, dtype=f32)
    return wm


def kernel(**inputs):
    x = np.asarray(inputs["L0_attn_x"], np.float32)
    wm = _host_layout(inputs)
    nc = build_program(32)
    in_maps = [dict(wm, x=np.ascontiguousarray(x[c])) for c in range(NCORES)]
    res = run_bass_kernel_spmd(nc, in_maps, core_ids=list(range(NCORES)))
    return np.stack([res.results[c]["y"] for c in range(NCORES)], axis=0)


# version 48
# speedup vs baseline: 1.2127x; 1.0131x over previous
import contextlib
import numpy as np
import concourse.bass as bass
import concourse.mybir as mybir
from concourse.bass_utils import run_bass_kernel_spmd

F32 = mybir.dt.float32
BF16 = mybir.dt.bfloat16
U8 = mybir.dt.uint8
AF = mybir.ActivationFunctionType
ALU = mybir.AluOpType
AX = mybir.AxisListType

NCORES = 8
D = 1024
C = 8
H = 16
HD = 64
E = 6
R = 16
SMALL = 118
CAT2 = 112
EPS = 1e-5
NB = 4
NBB = 5


class Builder:
    ENGINES = ("sync", "gpsimd", "vector", "scalar", "tensor")

    def __init__(self):
        self.ops = {e: [] for e in self.ENGINES}
        self.lab = {}
        self.seq = 0

    def op(self, eng, fn, waits=(), inc=None, n=1, label=None, key=None):
        w = tuple(x for x in waits if x is not None)
        if key is None:
            key = float(self.seq)
        rec = {"eng": eng, "fn": fn, "waits": w, "inc": inc,
               "n": n if inc is not None else 0, "label": label,
               "key": key, "seq": self.seq}
        self.seq += 1
        if label is not None:
            assert inc is not None and label not in self.lab
            self.lab[label] = rec
        self.ops[eng].append(rec)

    def L(self, name):
        return name if name in self.lab else None

    def resolve(self):
        for e in self.ENGINES:
            self.ops[e].sort(key=lambda r: (r["key"], r["seq"]))
            cnt = {}
            for r in self.ops[e]:
                if r["inc"] is not None:
                    cnt[r["inc"]] = cnt.get(r["inc"], 0) + r["n"]
                    r["val"] = cnt[r["inc"]]
        bad = []
        for e in self.ENGINES:
            for r in self.ops[e]:
                rw = []
                for wname in r["waits"]:
                    if isinstance(wname, tuple):
                        rw.append(wname)
                        continue
                    p = self.lab[wname]
                    if not (p["key"] < r["key"] or
                            (p["key"] == r["key"] and p["seq"] < r["seq"])):
                        bad.append((wname, p["key"], r["label"] or r["seq"], r["key"]))
                    rw.append((p["inc"], p["val"]))
                r["rw"] = rw
        assert not bad, f"wait edges not backward in key order: {bad[:10]}"


def build_program(NT):
    T = NT * 128
    nc = bass.Bass("TRN2", target_bir_lowering=False, debug=False)

    x_in = nc.dram_tensor("x", [T, D], F32, kind="ExternalInput").ap()
    y_out = nc.dram_tensor("y", [T, D], F32, kind="ExternalOutput").ap()
    wqT = nc.dram_tensor("wqT", [D, D], F32, kind="ExternalInput").ap()
    wkT = nc.dram_tensor("wkT", [D, D], F32, kind="ExternalInput").ap()
    wvT = nc.dram_tensor("wvT", [D, D], F32, kind="ExternalInput").ap()
    wbT = nc.dram_tensor("wbT", [D, D], F32, kind="ExternalInput").ap()
    wbd = nc.dram_tensor("wbd", [C, 128, 128], F32, kind="ExternalInput").ap()
    wsT = nc.dram_tensor("wsT", [D, SMALL], F32, kind="ExternalInput").ap()
    w2 = nc.dram_tensor("w2", [CAT2, D], F32, kind="ExternalInput").ap()
    bqr = nc.dram_tensor("bqr", [1, D], F32, kind="ExternalInput").ap()
    bkr = nc.dram_tensor("bkr", [1, D], F32, kind="ExternalInput").ap()
    bvr = nc.dram_tensor("bvr", [1, D], F32, kind="ExternalInput").ap()
    bbr = nc.dram_tensor("bbr", [1, D], F32, kind="ExternalInput").ap()
    b118 = nc.dram_tensor("b118", [1, SMALL], F32, kind="ExternalInput").ap()
    identd = nc.dram_tensor("identd", [128, 128], F32, kind="ExternalInput").ap()
    eq_sp = nc.dram_tensor("eq_sp", [T, D], BF16).ap()
    ek_sp = nc.dram_tensor("ek_sp", [T, D], BF16).ap()
    v_sp = nc.dram_tensor("v_sp", [T, D], F32).ap()
    cc_in = nc.dram_tensor("cc_in", [128, 2], F32).ap()
    cc_out = nc.dram_tensor("cc_out", [NCORES * 128, 2], F32).ap()

    es = contextlib.ExitStack()
    S = lambda n, sh, dt: es.enter_context(nc.sbuf_tensor(n, sh, dt))
    PS = lambda n, sh, dt: es.enter_context(nc.psum_tensor(n, sh, dt))

    s_wq = S("s_wq", [128, C, D], BF16)
    s_wk = S("s_wk", [128, C, D], BF16)
    s_wv = S("s_wv", [128, C, D], BF16)
    s_wb = S("s_wb", [128, C, D], BF16)
    s_wbd = S("s_wbd", [128, C, 128], BF16)
    s_ws = S("s_ws", [128, C, SMALL], BF16)
    s_w2 = S("s_w2", [CAT2, D], BF16)
    s_id = S("s_id", [128, 128], BF16)
    s_onesf = S("s_onesf", [128, 1], F32)

    s_accq = S("s_accq", [128, NT], F32)
    s_acck = S("s_acck", [128, NT], F32)
    s_nrm = S("s_nrm", [128, 2], F32)
    s_ag8 = S("s_ag8", [128, NCORES, 2], F32)
    s_S128 = S("s_S128", [128, 2], F32)
    s_lnS = S("s_lnS", [128, 2], F32)
    s_rn = S("s_rn", [128, 2], F32)
    s_t2 = S("s_t2", [128, 2], F32)
    s_ts = S("s_ts", [128, 1], F32)
    s_scq = S("s_scq", [128, 1], F32)
    s_scqk = S("s_scqk", [128, 1], F32)
    s_lnscqk = S("s_lnscqk", [128, 1], F32)
    s_onebc = S("s_onebc", [128, 1], BF16)
    s_row2b = S("s_row2b", [2, D], BF16)
    s_row2s = S("s_row2s", [2, SMALL], BF16)
    s_eps = S("s_eps", [128, 1], F32)

    def sz(cols, dt):
        return cols * (2 if dt == BF16 else 4)

    bset = [("x", D, F32), ("eq", D, BF16), ("ek", D, BF16), ("v", D, F32),
            ("z", D, F32), ("ao", D, F32),
            ("u2fm", D, BF16),
            ("cat1", SMALL, F32), ("cat2", CAT2, BF16), ("c2fm", 128, BF16),
            ("s", H, F32), ("r", H, F32), ("w", H, F32),
            ("g6", 8, F32), ("t8", 8, F32), ("k6", 8, F32), ("sw", 8, F32),
            ("tot", 1, F32), ("ngm", 1, F32), ("r2", 2, BF16), ("r3", 1, F32), ("r3sq", 1, F32), ("lnr3", 1, F32),
            ("st6", 12, F32), ("st2", 2, F32), ("lnv", 1, F32),
            ("rstd", 1, F32), ("nmr", 1, F32)]
    aset = [("x", D, F32), ("u1", D, BF16), ("u1fm", D, BF16),
            ("qsb", D, BF16), ("ksb", D, BF16), ("qfm", D, BF16),
            ("kfm", D, BF16), ("v", D, F32), ("eq", D, BF16), ("ek", D, BF16),
            ("st6", 12, F32), ("st2", 2, F32), ("lnv", 1, F32),
            ("rstd", 1, F32), ("nmr", 1, F32)]
    per_b = sum(((sz(c, d) + 31) & ~31) for _, c, d in bset)
    ARENA = NBB * (per_b + 64)
    arena = S("arena", [128, ARENA], U8)
    s_dump = arena[:, ARENA - 2048:ARENA].bitcast(BF16)
    s_bqb = arena[:, ARENA - 4096:ARENA - 2048].bitcast(BF16)
    s_bkb = arena[:, ARENA - 6144:ARENA - 4096].bitcast(BF16)
    s_bvb = arena[:, ARENA - 8192:ARENA - 6144].bitcast(BF16)

    class Carver:
        def __init__(self):
            self.off = 0

        def take(self, cols, dt):
            nb = sz(cols, dt)
            off = (self.off + 31) & ~31
            view = arena[:, off:off + nb].bitcast(dt)
            self.off = off + nb
            assert self.off <= ARENA
            return view

    cv = Carver()
    A = {name: [] for name, _, _ in aset}
    for i in range(NB):
        for name, cols, dt in aset:
            A[name].append(cv.take(cols, dt))
    assert cv.off <= ARENA - 8192, (cv.off, ARENA)
    cv = Carver()
    B = {name: [] for name, _, _ in bset}
    for i in range(NBB):
        for name, cols, dt in bset:
            ap = cv.take(cols, dt)
            if name == "c2fm":
                ap = ap[0:CAT2, :]
            B[name].append(ap)

    a_x, a_u1, a_u1fm = A["x"], A["u1"], A["u1fm"]
    a_qsb, a_ksb, a_qfm, a_kfm = A["qsb"], A["ksb"], A["qfm"], A["kfm"]
    a_v, a_eq, a_ek = A["v"], A["eq"], A["ek"]
    a_st6, a_st2, a_lnv, a_rstd, a_nmr = A["st6"], A["st2"], A["lnv"], A["rstd"], A["nmr"]
    b_x, b_eq, b_ek, b_v = B["x"], B["eq"], B["ek"], B["v"]
    b_z, b_ao = B["z"], B["ao"]
    b_e = B["eq"]
    b_x1 = b_v
    b_u2 = b_ek
    b_u2fm = B["u2fm"]
    b_cat1, b_cat2, b_c2fm = B["cat1"], B["cat2"], B["c2fm"]
    b_s, b_r, b_w = B["s"], B["r"], B["w"]
    b_g6, b_t8, b_k6, b_sw = B["g6"], B["t8"], B["k6"], B["sw"]
    b_tot, b_ngm, b_r2, b_r3 = B["tot"], B["ngm"], B["r2"], B["r3"]
    b_r3sq, b_lnr3 = B["r3sq"], B["lnr3"]
    b_st6, b_st2, b_lnv, b_rstd, b_nmr = B["st6"], B["st2"], B["lnv"], B["rstd"], B["nmr"]

    b_r2T = [S(f"b_r2T{i}", [2, 128], BF16) for i in range(NBB)]

    pQ = PS("pQ", [128, D], F32)
    pK = PS("pK", [128, D], F32)
    pV = PS("pV", [128, D], F32)
    pT6 = PS("pT6", [128, D], BF16)
    pT7 = PS("pT7", [128, D], BF16)

    sem_names = (["sA_x%d" % i for i in range(NB)] +
                 ["sA_sp%d" % i for i in range(NB)] +
                 ["sB_in%d" % i for i in range(NBB)] +
                 ["sB_v%d" % i for i in range(NBB)] +
                 ["sB_out%d" % i for i in range(NBB)] +
                 ["sWa", "sWid", "sWq", "sWk", "sWv", "sWqb", "sWkb", "sWvb",
                  "sWffn", "sCC", "sCCd", "sCCg",
                  "sV", "sS", "sT", "sP"])
    sems = {n: es.enter_context(nc.semaphore(n)) for n in sem_names}

    b = Builder()
    L = b.L

    wk = [-100]
    def wload(name, dst, src, sem, label=None):
        wk[0] += 1
        b.op("gpsimd", lambda g, d=dst, s=src: g.dma_start(out=d, in_=s),
             inc=sem, n=16, label=label, key=wk[0])

    wload("w_id", s_id[:], identd, "sWid", label="w_id")
    wload("w_wq", s_wq[:], wqT.rearrange("(c p) o -> p c o", p=128), "sWq", label="w_wq")
    wload("w_wk", s_wk[:], wkT.rearrange("(c p) o -> p c o", p=128), "sWk", label="w_wk")
    wload("w_wv", s_wv[:], wvT.rearrange("(c p) o -> p c o", p=128), "sWv", label="w_wv")
    wload("w_wbd", s_wbd[:], wbd.rearrange("c p m -> p c m"), "sWa", label="w_a")
    wload("w_bqb", s_bqb[:], bqr.broadcast_to([128, D]), "sWqb", label="w_bqb")
    wload("w_bkb", s_bkb[:], bkr.broadcast_to([128, D]), "sWkb", label="w_bkb")
    wload("w_bvb", s_bvb[:], bvr.broadcast_to([128, D]), "sWvb", label="w_bvb")
    wload("w_wb", s_wb[:], wbT.rearrange("(c p) o -> p c o", p=128), "sWffn")
    wload("w_ws", s_ws[:], wsT.rearrange("(c p) j -> p c j", p=128), "sWffn")
    wload("w_w2", s_w2[:], w2, "sWffn", label="w_ffn")

    b.op("vector", lambda v: v.memset(s_eps[:], EPS), inc="sV", label="v_eps", key=-79)
    b.op("vector", lambda v: v.memset(s_onesf[:], 1.0), inc="sV", label="v_onesf", key=-78)
    b.op("vector", lambda v: v.memset(s_onebc[:], 1.0), inc="sV", label="v_onebc", key=-77.5)
    b.op("vector", lambda v: v.memset(s_nrm[:], 0.0), inc="sV", label="v_z0", key=-77.4)
    b.op("sync", lambda s: s.dma_start(out=cc_in, in_=s_nrm[:]),
         waits=["v_z0"], inc="sCCd", n=16, label="g_ccin0", key=-77.2)
    b.op("gpsimd", lambda g: g.collective_compute(
            "AllGather", ALU.bypass, replica_groups=[list(range(NCORES))],
            ins=[cc_in.opt()], outs=[cc_out.opt()]),
         waits=["g_ccin0"], inc="sCC", n=1, label="g_ccw", key=-77)
    b.op("gpsimd", lambda g: g.dma_start(out=s_row2b[1:2, :], in_=bbr),
         inc="sWffn", n=16, label="w_r2b", key=-76)
    b.op("gpsimd", lambda g: g.dma_start(out=s_row2s[1:2, :], in_=b118),
         inc="sWffn", n=16, label="w_r2s", key=-75)

    IIA = 20.0
    for t in range(NT):
        p = t % NB
        rs = slice(t * 128, (t + 1) * 128)
        ka = lambda s, t=t: t * IIA + s

        b.op("sync", lambda s, p=p, rs=rs: s.dma_start(out=a_x[p][:], in_=x_in[rs, :]),
             waits=[L(f"s_u1n{t-NB}"), L(f"v_bn1_{t-NB}")],
             inc=f"sA_x{p}", n=16, label=f"ax{t}", key=ka(-25.0))

        b.op("vector", lambda v, p=p: v.bn_stats(a_st6[p][:, 0:6], a_x[p][:, 0:512]),
             waits=[f"ax{t}", L(f"v_agg1_{t-NB}")], inc="sV", key=ka(-15.0))
        b.op("vector", lambda v, p=p: v.bn_stats(a_st6[p][:, 6:12], a_x[p][:, 512:1024]),
             waits=[], inc="sV", label=f"v_bn1_{t}", key=ka(-14.9))
        b.op("vector", lambda v, p=p: v.bn_aggr(a_st2[p][:], a_st6[p][:]),
             waits=[f"v_bn1_{t}", L(f"s_ln1_{t-NB}")], inc="sV", label=f"v_agg1_{t}", key=ka(-14.0))
        b.op("scalar", lambda s, p=p: s.activation(a_lnv[p][:], a_st2[p][:, 1:2], AF.Ln, bias=s_eps[:]),
             waits=[f"v_agg1_{t}", L(f"s_rstd1_{t-NB}"), "v_eps"],
             inc="sS", label=f"s_ln1_{t}", key=ka(-11.4))
        b.op("scalar", lambda s, p=p: s.activation(a_rstd[p][:], a_lnv[p][:], AF.Exp, scale=-0.5),
             waits=[f"s_ln1_{t}", L(f"v_nmr1_{t-NB}"), L(f"s_u1n{t-NB}")],
             inc="sS", label=f"s_rstd1_{t}", key=ka(-11.2))
        b.op("vector", lambda v, p=p: v.tensor_scalar(
                a_nmr[p][:], a_st2[p][:, 0:1], a_rstd[p][:], -1.0, ALU.mult, ALU.mult),
             waits=[f"s_rstd1_{t}", L(f"s_u1n{t-NB}")], inc="sV", label=f"v_nmr1_{t}", key=ka(-7.5))
        b.op("scalar", lambda s, p=p: s.activation(
                a_u1[p][:], a_x[p][:], AF.Identity, bias=a_nmr[p][:], scale=a_rstd[p][:]),
             waits=[f"v_nmr1_{t}", L(f"t_Tu1_{t-NB}")], inc="sS", label=f"s_u1n{t}", key=ka(-7.2))

        def tu1(te, p=p):
            for c in range(C):
                ins = te.transpose(pT6[:, c * 128:(c + 1) * 128],
                                   a_u1[p][:, c * 128:(c + 1) * 128], s_id[:])
            return ins
        b.op("tensor", tu1, waits=[f"s_u1n{t}", L(f"s_u1fm_{t-1}"), "w_id"],
             inc="sT", label=f"t_Tu1_{t}", key=ka(7.0))
        b.op("scalar", lambda s, p=p: s.copy(a_u1fm[p][:], pT6[:]),
             waits=[f"t_Tu1_{t}", L(f"t_v_{t-NB}")], inc="sS", label=f"s_u1fm_{t}", key=ka(8.5))

        def proj(te, pdst, wsb, p=p):
            for h in range(2):
                cs = slice(h * 512, (h + 1) * 512)
                for c in range(C):
                    ins = te.matmul(pdst[:, cs], a_u1fm[p][:, c * 128:(c + 1) * 128],
                                    wsb[:, c, cs], start=(c == 0), stop=(c == C - 1),
                                    skip_group_check=True)
            return ins
        b.op("tensor", lambda te, p=p: proj(te, pQ[:], s_wq, p),
             waits=[f"s_u1fm_{t}", L(f"s_qev_{t-1}"), "w_wq"],
             inc="sT", label=f"t_q_{t}", key=ka(10.0))
        b.op("tensor", lambda te, p=p: proj(te, pK[:], s_wk, p),
             waits=[L(f"s_ekev_{t-1}"), "w_wk"], inc="sT", label=f"t_k_{t}", key=ka(11.5))
        b.op("tensor", lambda te, p=p: proj(te, pV[:], s_wv, p),
             waits=[L(f"s_eqev_{t-1}"), "w_wv"], inc="sT", label=f"t_v_{t}", key=ka(13.5))

        b.op("vector", lambda v, p=p: v.tensor_add(a_qsb[p][:], pQ[:], s_bqb[:]),
             waits=[f"t_q_{t}", "w_bqb", L(f"v_ssq_{t-NB}"), L(f"t_Tq_{t-NB}")],
             inc="sV", label=f"s_qev_{t}", key=ka(12.0))
        b.op("vector", lambda v, p=p: v.tensor_add(a_ksb[p][:], pK[:], s_bkb[:]),
             waits=[f"t_k_{t}", "w_bkb", L(f"v_ssk_{t-NB}"), L(f"t_Tk_{t-NB}")],
             inc="sV", label=f"s_kev_{t}", key=ka(13.5))
        b.op("vector", lambda v, p=p: v.tensor_add(a_v[p][:], pV[:], s_bvb[:]),
             waits=[f"t_v_{t}", "w_bvb", L(f"g_v_{t-NB}")], inc="sV", label=f"v_vev_{t}", key=ka(15.0))

        b.op("vector", lambda v, p=p, t=t: v.scalar_tensor_tensor(
                s_dump[:], a_qsb[p][:], 1.0, a_qsb[p][:], ALU.bypass, ALU.mult,
                accum_out=s_accq[:, t:t + 1]),
             waits=[f"s_qev_{t}", f"v_vev_{t}"], inc="sV", label=f"v_ssq_{t}", key=ka(15.5))
        b.op("vector", lambda v, p=p, t=t: v.scalar_tensor_tensor(
                s_dump[:], a_ksb[p][:], 1.0, a_ksb[p][:], ALU.bypass, ALU.mult,
                accum_out=s_acck[:, t:t + 1]),
             waits=[f"s_kev_{t}", f"v_ssq_{t}"], inc="sV", label=f"v_ssk_{t}", key=ka(15.8))

        def tq(te, p=p):
            for c in range(C):
                ins = te.transpose(pT7[:, c * 128:(c + 1) * 128],
                                   a_qsb[p][:, c * 128:(c + 1) * 128], s_id[:])
            return ins
        b.op("tensor", tq, waits=[f"s_qev_{t}", L(f"s_kfm_{t-1}")],
             inc="sT", label=f"t_Tq_{t}", key=ka(13.0))
        def tk(te, p=p):
            for c in range(C):
                ins = te.transpose(pT7[:, c * 128:(c + 1) * 128],
                                   a_ksb[p][:, c * 128:(c + 1) * 128], s_id[:])
            return ins
        b.op("tensor", tk, waits=[f"s_kev_{t}", f"s_qfm_{t}"],
             inc="sT", label=f"t_Tk_{t}", key=ka(14.2))
        b.op("scalar", lambda s, p=p: s.copy(a_qfm[p][:], pT7[:]),
             waits=[f"t_Tq_{t}", L(f"t_eq_{t-NB}")], inc="sS", label=f"s_qfm_{t}", key=ka(13.4))
        b.op("scalar", lambda s, p=p: s.copy(a_kfm[p][:], pT7[:]),
             waits=[f"t_Tk_{t}", L(f"t_ek_{t-NB}")], inc="sS", label=f"s_kfm_{t}", key=ka(14.6))

        def eqmm(te, src, pdst, p=p):
            for c in range(C):
                ins = te.matmul(pdst[:, c * 128:(c + 1) * 128],
                                src[p][:, c * 128:(c + 1) * 128],
                                s_wbd[:, c, :], start=(c % 4 == 0),
                                stop=(c % 4 == 3), skip_group_check=True)
            return ins
        b.op("tensor", lambda te, p=p: eqmm(te, a_qfm, pV[:], p),
             waits=[f"s_qfm_{t}", f"v_vev_{t}", "w_a"], inc="sT", label=f"t_eq_{t}", key=ka(27.3))
        b.op("tensor", lambda te, p=p: eqmm(te, a_kfm, pK[:], p),
             waits=[f"s_kfm_{t}", f"s_kev_{t}"], inc="sT", label=f"t_ek_{t}", key=ka(27.7))
        b.op("scalar", lambda s, p=p: s.copy(a_eq[p][:], pV[:]),
             waits=[f"t_eq_{t}", L(f"g_v_{t-NB}")], inc="sS", label=f"s_eqev_{t}", key=ka(28.6))
        b.op("scalar", lambda s, p=p: s.copy(a_ek[p][:], pK[:]),
             waits=[f"t_ek_{t}", L(f"g_v_{t-NB}")], inc="sS", label=f"s_ekev_{t}", key=ka(28.9))

        b.op("gpsimd", lambda g, p=p, rs=rs: g.dma_start(out=eq_sp[rs, :], in_=a_eq[p][:]),
             waits=[f"s_eqev_{t}"], inc=f"sA_sp{p}", n=16, label=f"g_eq_{t}", key=ka(29.45))
        b.op("gpsimd", lambda g, p=p, rs=rs: g.dma_start(out=ek_sp[rs, :], in_=a_ek[p][:]),
             waits=[f"s_ekev_{t}"], inc=f"sA_sp{p}", n=16, label=f"g_ek_{t}", key=ka(29.55))
        b.op("gpsimd", lambda g, p=p, rs=rs: g.dma_start(out=v_sp[rs, :], in_=a_v[p][:]),
             waits=[f"v_vev_{t}"], inc=f"sA_sp{p}", n=16, label=f"g_v_{t}", key=ka(29.8))

    TL = NT - 1
    FZ = NT * IIA + 25.0
    b.op("vector", lambda v: v.reduce_sum(s_nrm[:, 0:1], s_accq[:], axis=AX.X),
         waits=[f"v_ssq_{TL}"], inc="sV", label="vf_q", key=FZ + 0)
    b.op("vector", lambda v: v.reduce_sum(s_nrm[:, 1:2], s_acck[:], axis=AX.X),
         waits=[f"v_ssk_{TL}"], inc="sV", label="vf_k", key=FZ + 1)
    b.op("sync", lambda g: g.dma_start(out=cc_in, in_=s_nrm[:]),
         waits=["vf_k", "g_ccw"], inc="sCCd", n=16, label="g_ccin", key=FZ + 2)
    b.op("gpsimd", lambda g: g.collective_compute(
            "AllGather", ALU.bypass, replica_groups=[list(range(NCORES))],
            ins=[cc_in.opt()], outs=[cc_out.opt()]),
         waits=["g_ccin"], inc="sCC", n=1, label="g_cc", key=FZ + 3)
    for r in range(NCORES):
        b.op("gpsimd", lambda g, r=r: g.dma_start(
                out=s_ag8[:, r, :], in_=cc_out[r * 128:(r + 1) * 128, :]),
             waits=["g_cc"], inc="sCCg", n=16,
             label=f"g_ag{r}", key=FZ + 4 + r * 0.01)
    b.op("vector", lambda v: v.reduce_sum(
            s_S128[:], s_ag8[:].rearrange("p r c -> p c r"), axis=AX.X),
         waits=[f"g_ag{NCORES-1}"], inc="sV", label="v_S128", key=FZ + 5)
    b.op("scalar", lambda s: s.activation(s_lnS[:], s_S128[:], AF.Ln),
         waits=["v_S128"], inc="sS", label="s_lnS", key=FZ + 14)
    b.op("scalar", lambda s: s.activation(s_rn[:], s_lnS[:], AF.Exp, scale=-0.5),
         waits=["s_lnS"], inc="sS", label="s_rn", key=FZ + 15)
    b.op("vector", lambda v: v.tensor_mul(s_t2[:], s_S128[:], s_rn[:]),
         waits=["s_rn"], inc="sV", label="v_t2a", key=FZ + 16)
    b.op("vector", lambda v: v.tensor_mul(s_t2[:], s_t2[:], s_rn[:]),
         waits=["v_t2a"], inc="sV", label="v_t2b", key=FZ + 17)
    b.op("vector", lambda v: v.reduce_sum(s_ts[:], s_t2[:], axis=AX.X),
         waits=["v_t2b"], inc="sV", label="v_ts", key=FZ + 18)
    b.op("scalar", lambda s: s.activation(s_scq[:], s_ts[:], AF.Exp, scale=-0.5),
         waits=["v_ts"], inc="sS", label="s_scq", key=FZ + 19)
    b.op("vector", lambda v: v.tensor_scalar(s_scqk[:], s_scq[:], 1.0 / 64.0, None, ALU.mult),
         waits=["s_scq"], inc="sV", label="v_scqk", key=FZ + 20)
    b.op("scalar", lambda s: s.activation(s_lnscqk[:], s_scqk[:], AF.Ln),
         waits=["v_scqk"], inc="sS", label="s_lnscqk", key=FZ + 20.5)

    def csb(te):
        for h in range(2):
            cs = slice(h * 512, (h + 1) * 512)
            for c in range(C):
                ins = te.matmul(pQ[0:1, cs], s_onebc[:], s_wb[:, c, cs],
                                start=(c == 0), stop=(c == C - 1),
                                skip_group_check=True)
        return ins
    b.op("tensor", csb, waits=[f"s_qev_{TL}", "w_r2s", "v_onebc"],
         inc="sT", label="t_csb", key=FZ + 22)
    def css(te):
        for c in range(C):
            ins = te.matmul(pK[0:1, 0:SMALL], s_onebc[:], s_ws[:, c, :],
                            start=(c == 0), stop=(c == C - 1),
                            skip_group_check=True)
        return ins
    b.op("tensor", css, waits=[f"s_ekev_{TL}", "w_r2s"],
         inc="sT", label="t_css", key=FZ + 22.2)
    b.op("scalar", lambda s: s.copy(s_row2b[0:1, :], pQ[0:1, :]),
         waits=["t_csb"], inc="sS", label="s_csbev", key=FZ + 22.5)
    b.op("scalar", lambda s: s.copy(s_row2s[0:1, :], pK[0:1, 0:SMALL]),
         waits=["t_css"], inc="sS", label="s_cssev", key=FZ + 23.5)

    adone = ["vf_k", f"s_ekev_{TL}", f"t_ek_{TL}"] + \
            [L(f"g_v_{TL - i}") for i in range(min(NB, NT))]
    for i in range(NBB):
        b.op("vector", lambda v, i=i: v.memset(b_g6[i][:], -1e30),
             waits=adone, inc="sV", label=f"v_mg{i}", key=FZ + 21 + i * 0.1)

    IIB = 11.0
    BZ = FZ + 30.0
    for t in range(NT):
        p = t % NBB
        pp = t & 1
        kb = lambda s, t=t: BZ + t * IIB + s
        rs = slice(t * 128, (t + 1) * 128)
        pF = pQ if pp == 0 else pK
        pSm = pV[:, 0:SMALL] if pp == 0 else pV[:, 512:512 + SMALL]
        pT = pT6 if pp == 0 else pT7
        first = adone if t < NBB else []

        b.op("sync", lambda s, p=p, rs=rs: s.dma_start(out=b_x[p][:], in_=x_in[rs, :]),
             waits=[L(f"g_out_{t-NBB}")] + first, inc=f"sB_in{p}", n=16, label=f"b_lx_{t}", key=kb(6.4))
        b.op("sync", lambda s, p=p, rs=rs: s.dma_start(out=b_eq[p][:], in_=eq_sp[rs, :]),
             waits=[f"g_v_{t}", L(f"v_sraw_{t-NBB}")] + first, inc=f"sB_in{p}", n=16, label=f"b_leq_{t}", key=kb(6.5))
        b.op("sync", lambda s, p=p, rs=rs: s.dma_start(out=b_ek[p][:], in_=ek_sp[rs, :]),
             waits=[L(f"t_Tu2_{t-NBB}")] + first, inc=f"sB_in{p}", n=16, label=f"b_lek_{t}", key=kb(6.6))
        b.op("gpsimd", lambda g, p=p, rs=rs: g.dma_start(out=b_v[p][:], in_=v_sp[rs, :]),
             waits=[f"g_v_{t}", L(f"po_x2_{t-NBB}")] + first, inc=f"sB_v{p}", n=16,
             label=f"b_lv_{t}", key=kb(6.7) if t < NBB else kb(-13.5))

        e12 = b_z[p].bitcast(BF16)
        b.op("scalar", lambda s, p=p, e12=e12: s.activation(
                e12[:, 0:D], b_eq[p][:], AF.Exp, scale=s_rn[:, 0:1],
                bias=s_lnscqk[:]),
             waits=[f"b_lek_{t}", "s_rn", "s_lnscqk", L(f"po_x1_{t-NBB}")],
             inc="sS", label=f"s_e1_{t}", key=kb(8.0))
        b.op("scalar", lambda s, p=p, e12=e12: s.activation(
                e12[:, D:2 * D], b_ek[p][:], AF.Exp, scale=s_rn[:, 1:2]),
             waits=[], inc="sS", label=f"s_e2_{t}", key=kb(8.3))
        b.op("vector", lambda v, p=p, e12=e12: v.tensor_mul(
                b_e[p][:], e12[:, 0:D], e12[:, D:2 * D]),
             waits=[f"s_e2_{t}", L(f"v_sraw_{t-NBB}")], inc="sV", label=f"s_e_{t}", key=kb(15.0))
        b.op("vector", lambda v, p=p: v.reduce_sum(
                b_s[p][:], b_e[p][:].rearrange("t (h m) -> t h m", h=H), axis=AX.X),
             waits=[f"s_e_{t}", L(f"v_w_{t-NBB}")], inc="sV", label=f"v_sraw_{t}", key=kb(16.5))
        b.op("vector", lambda v, p=p: v.reciprocal(b_r[p][:], b_s[p][:]),
             waits=[f"v_sraw_{t}"], inc="sV", label=f"v_recip_{t}", key=kb(17.2))
        b.op("vector", lambda v, p=p: v.tensor_mul(b_w[p][:], b_s[p][:], b_r[p][:]),
             waits=[f"v_recip_{t}"], inc="sV", label=f"v_w_{t}", key=kb(17.4))
        b.op("gpsimd", lambda g, p=p: g.tensor_tensor(
                b_ao[p][:, 0:512].rearrange("t (h m) -> t h m", h=H // 2),
                b_v[p][:, 0:512].rearrange("t (h m) -> t h m", h=H // 2),
                b_w[p][:, 0:8].unsqueeze(2).broadcast_to([128, H // 2, HD]), ALU.mult),
             waits=[f"b_lv_{t}", f"v_w_{t}", L(f"s_ln2o_{t-NBB}"), L(f"v_bn2_{t-NBB}")],
             inc="sP", label=f"po_aoa_{t}", key=kb(18.0))
        b.op("gpsimd", lambda g, p=p: g.tensor_tensor(
                b_ao[p][:, 512:1024].rearrange("t (h m) -> t h m", h=H // 2),
                b_v[p][:, 512:1024].rearrange("t (h m) -> t h m", h=H // 2),
                b_w[p][:, 8:16].unsqueeze(2).broadcast_to([128, H // 2, HD]), ALU.mult),
             waits=[f"b_lv_{t}", f"v_w_{t}"], inc="sP", label=f"po_ao_{t}", key=kb(18.3))

        b.op("vector", lambda v, p=p: v.bn_stats(b_st6[p][:, 0:6], b_ao[p][:, 0:512]),
             waits=[f"po_aoa_{t}", L(f"v_agg4_{t-NBB}")], inc="sV", key=kb(19.0))
        b.op("vector", lambda v, p=p: v.bn_stats(b_st6[p][:, 6:12], b_ao[p][:, 512:1024]),
             waits=[f"po_ao_{t}"], inc="sV", label=f"v_bn2_{t}", key=kb(19.1))
        b.op("vector", lambda v, p=p: v.bn_aggr(b_st2[p][:], b_st6[p][:]),
             waits=[f"v_bn2_{t}", L(f"s_ln4_{t-NBB}")], inc="sV", label=f"v_agg2_{t}", key=kb(19.5))
        b.op("scalar", lambda s, p=p: s.activation(b_lnv[p][:], b_st2[p][:, 1:2], AF.Ln, bias=s_eps[:]),
             waits=[f"v_agg2_{t}", L(f"s_rstd4_{t-NBB}"), "v_eps"],
             inc="sS", label=f"s_ln2_{t}", key=kb(20.0))
        b.op("scalar", lambda s, p=p: s.activation(b_rstd[p][:], b_lnv[p][:], AF.Exp, scale=-0.5),
             waits=[f"s_ln2_{t}", L(f"v_nmr4_{t-NBB}"), L(f"s_ln4o_{t-NBB}")],
             inc="sS", label=f"s_rstd2_{t}", key=kb(20.3))
        b.op("vector", lambda v, p=p: v.tensor_scalar(
                b_nmr[p][:], b_st2[p][:, 0:1], b_rstd[p][:], -1.0, ALU.mult, ALU.mult),
             waits=[f"s_rstd2_{t}", L(f"s_ln4o_{t-NBB}")], inc="sV", label=f"v_nmr2_{t}", key=kb(20.6))
        b.op("scalar", lambda s, p=p: s.activation(
                b_z[p][:, 0:512], b_ao[p][:, 0:512], AF.Identity,
                bias=b_nmr[p][:], scale=b_rstd[p][:]),
             waits=[f"v_nmr2_{t}", f"s_e_{t}", f"s_e2_{t}"], inc="sS",
             label=f"s_ln2oa_{t}", key=kb(21.0))
        b.op("scalar", lambda s, p=p: s.activation(
                b_z[p][:, 512:1024], b_ao[p][:, 512:1024], AF.Identity,
                bias=b_nmr[p][:], scale=b_rstd[p][:]),
             waits=[], inc="sS", label=f"s_ln2o_{t}", key=kb(21.3))
        b.op("gpsimd", lambda g, p=p: g.tensor_add(
                b_x1[p][:, 0:512], b_x[p][:, 0:512], b_z[p][:, 0:512]),
             waits=[f"s_ln2oa_{t}", f"b_lek_{t}", L(f"po_x2_{t-NBB}"),
                    L(f"s_x1c_{t-NBB}"), L(f"v_bn3_{t-NBB}")],
             inc="sP", label=f"po_x1a_{t}", key=kb(22.5))
        b.op("gpsimd", lambda g, p=p: g.tensor_add(
                b_x1[p][:, 512:1024], b_x[p][:, 512:1024], b_z[p][:, 512:1024]),
             waits=[f"s_ln2o_{t}"], inc="sP", label=f"po_x1_{t}", key=kb(22.8))

        b.op("vector", lambda v, p=p: v.bn_stats(b_st6[p][:, 0:6], b_x1[p][:, 0:512]),
             waits=[f"po_x1a_{t}", f"v_agg2_{t}"], inc="sV", key=kb(24.0))
        b.op("vector", lambda v, p=p: v.bn_stats(b_st6[p][:, 6:12], b_x1[p][:, 512:1024]),
             waits=[f"po_x1_{t}"], inc="sV", label=f"v_bn3_{t}", key=kb(24.1))
        b.op("vector", lambda v, p=p: v.bn_aggr(b_st2[p][:], b_st6[p][:]),
             waits=[f"v_bn3_{t}", f"s_ln2_{t}"], inc="sV", label=f"v_agg3_{t}", key=kb(24.5))
        b.op("scalar", lambda s, p=p: s.activation(b_lnv[p][:], b_st2[p][:, 1:2], AF.Ln, bias=s_eps[:]),
             waits=[f"v_agg3_{t}", f"s_rstd2_{t}"], inc="sS", label=f"s_ln3_{t}", key=kb(25.0))
        b.op("scalar", lambda s, p=p: s.activation(b_r3[p][:], b_lnv[p][:], AF.Exp, scale=-0.5),
             waits=[f"s_ln3_{t}", L(f"v_ngs_{t-NBB}"), L(f"s_exg_{t-NBB}")],
             inc="sS", label=f"s_rstd3_{t}", key=kb(25.3))
        b.op("scalar", lambda s, p=p: s.activation(
                b_r2[p][:, 0:1], b_st2[p][:, 0:1], AF.Identity, scale=-1.0),
             waits=[f"s_rstd3_{t}", L(f"t_Trow_{t-NBB}")], inc="sS",
             label=f"s_nmu3_{t}", key=kb(25.5))
        b.op("scalar", lambda s, p=p: s.activation(
                b_r2[p][:, 1:2], b_lnv[p][:], AF.Exp, scale=0.5),
             waits=[f"s_nmu3_{t}"], inc="sS", label=f"s_sg3_{t}", key=kb(25.65))
        b.op("scalar", lambda s, p=p: s.activation(
                b_r3sq[p][:], b_lnv[p][:], AF.Exp, scale=-1.0),
             waits=[f"s_sg3_{t}", L(f"s_ln4_{t-NBB}")], inc="sS",
             label=f"v_r3sq_{t}", key=kb(25.7))
        b.op("scalar", lambda s, p=p: s.activation(
                b_lnr3[p][:], b_lnv[p][:], AF.Identity, scale=-0.5),
             waits=[f"v_r3sq_{t}", L(f"s_rstd4_{t-NBB}")], inc="sS",
             label=f"v_lnr3_{t}", key=kb(25.75))
        b.op("scalar", lambda s, p=p: s.copy(b_u2[p][:], b_x1[p][:]),
             waits=[f"po_x1a_{t}", f"po_x1_{t}", f"s_e2_{t}", f"s_e_{t}"], inc="sS",
             label=f"s_x1c_{t}", key=kb(23.3))

        def tu2(te, p=p, pT=pT):
            for c in range(C):
                ins = te.transpose(pT[:, c * 128:(c + 1) * 128],
                                   b_u2[p][:, c * 128:(c + 1) * 128], s_id[:])
            return ins
        tb1waits = [f"s_x1c_{t}", L(f"s_c2fm_{t-2}"), L(f"s_rowev_{t-1}")]
        if t == 0:
            tb1waits.append(f"s_u1fm_{TL}")
        if t == 1:
            tb1waits.append(f"s_kfm_{TL}")
        b.op("tensor", tu2, waits=tb1waits, inc="sT", label=f"t_Tu2_{t}", key=kb(27.5))
        b.op("vector", lambda v, p=p, pT=pT: v.tensor_scalar(
                b_u2fm[p][:], pT[:], 0.0, None, ALU.add),
             waits=[f"t_Tu2_{t}", L(f"t_base_{t-NBB}")], inc="sV", label=f"s_u2fm_{t}", key=kb(28.5))

        def small1(te, p=p, pSm=pSm):
            for c in range(C):
                ins = te.matmul(pSm, b_u2fm[p][:, c * 128:(c + 1) * 128],
                                s_ws[:, c, :], start=(c == 0), stop=False,
                                skip_group_check=True)
            return ins
        s1waits = [f"s_u2fm_{t}", L(f"s_cat1_{t-2}"), "w_r2s"]
        if t <= 1:
            s1waits.append(f"s_eqev_{TL}")
        b.op("tensor", small1, waits=s1waits, inc="sT", label=f"t_s1_{t}", key=kb(29.5))

        pTo = pT7 if pp == 0 else pT6
        trwaits = [f"s_sg3_{t}", L(f"s_c2fm_{t-1}")]
        if t == 0:
            trwaits.append(f"s_kfm_{TL}")
        b.op("tensor", lambda te, p=p, pTo=pTo: te.transpose(
                pTo[0:2, 0:128], b_r2[p][:], s_id[:]),
             waits=trwaits, inc="sT", label=f"t_Trow_{t}", key=kb(29.6))
        b.op("scalar", lambda s, p=p, pTo=pTo: s.copy(b_r2T[p][:], pTo[0:2, 0:128]),
             waits=[f"t_Trow_{t}", L(f"t_fix_{t-NBB}")], inc="sS",
             label=f"s_rowev_{t}", key=kb(29.7))
        b.op("tensor", lambda te, p=p, pSm=pSm: te.matmul(
                pSm, b_r2T[p][:], s_row2s[:], start=False, stop=True,
                skip_group_check=True),
             waits=[f"s_rowev_{t}", "s_cssev"], inc="sT", label=f"t_s1f_{t}", key=kb(29.8))

        def base(te, p=p, pF=pF):
            for h in range(2):
                cs = slice(h * 512, (h + 1) * 512)
                for c in range(C):
                    ins = te.matmul(pF[:, cs], b_u2fm[p][:, c * 128:(c + 1) * 128],
                                    s_wb[:, c, cs], start=(c == 0), stop=False,
                                    skip_group_check=True)
            return ins
        bwaits = [f"s_u2fm_{t}", L(f"s_ln4o_{t-2}"), "w_r2s"]
        if t == 0:
            bwaits.append("s_csbev")
        if t == 1:
            bwaits.append("s_cssev")
        b.op("tensor", base, waits=bwaits, inc="sT", label=f"t_base_{t}", key=kb(30.0))
        def fix(te, p=p, pF=pF):
            for h in range(2):
                cs = slice(h * 512, (h + 1) * 512)
                ins = te.matmul(pF[:, cs], b_r2T[p][:], s_row2b[:, cs],
                                start=False, stop=False, skip_group_check=True)
            return ins
        b.op("tensor", fix, waits=[f"s_rowev_{t}", "s_csbev"],
             inc="sT", label=f"t_fix_{t}", key=kb(30.2))

        b.op("scalar", lambda s, p=p, pSm=pSm: s.copy(b_cat1[p][:], pSm),
             waits=[f"t_s1f_{t}", L(f"v_shc_{t-NBB}")], inc="sS", label=f"s_cat1_{t}", key=kb(30.5))
        b.op("vector", lambda v, p=p: v.reduce_max(
                b_ngm[p][:], b_cat1[p][:, 112:118], axis=AX.X, negate=True),
             waits=[f"s_cat1_{t}", L(f"s_exg_{t-NBB}")], inc="sV", label=f"v_ngm_{t}", key=kb(31.0))
        b.op("vector", lambda v, p=p: v.tensor_scalar(
                b_ngm[p][:], b_ngm[p][:], b_r3[p][:], None, ALU.mult),
             waits=[f"v_ngm_{t}", f"s_rstd3_{t}"], inc="sV", label=f"v_ngs_{t}", key=kb(31.1))
        b.op("scalar", lambda s, p=p: s.activation(
                b_g6[p][:, 0:6], b_cat1[p][:, 112:118], AF.Exp, bias=b_ngm[p][:],
                scale=b_r3[p][:], accum_out=b_tot[p][:]),
             waits=[f"v_ngs_{t}", L(f"v_gate_{t-NBB}"), L(f"v_top8_{t-NBB}"),
                    L(f"v_rt_{t-NBB}"), f"v_mg{p}"],
             inc="sS", label=f"s_exg_{t}", key=kb(31.3))
        b.op("vector", lambda v, p=p: v.reciprocal(b_ngm[p][:], b_tot[p][:]),
             waits=[f"s_exg_{t}"], inc="sV", label=f"v_grec_{t}", key=kb(31.8))
        b.op("vector", lambda v, p=p: v.tensor_scalar(
                b_g6[p][:, 0:6], b_g6[p][:, 0:6], b_ngm[p][:], None, ALU.mult),
             waits=[f"v_grec_{t}"], inc="sV", label=f"v_gate_{t}", key=kb(32.0))
        b.op("vector", lambda v, p=p: v.max(b_t8[p][:], b_g6[p][:]),
             waits=[f"v_gate_{t}", L(f"v_swt_{t-NBB}")], inc="sV", label=f"v_top8_{t}", key=kb(32.2))
        b.op("vector", lambda v, p=p: v.scalar_tensor_tensor(
                b_sw[p][:, 0:6], b_g6[p][:, 0:6], b_t8[p][:, 1:2],
                b_g6[p][:, 0:6], ALU.is_ge, ALU.mult, accum_out=b_tot[p][:]),
             waits=[f"v_top8_{t}", L(f"v_spw_{t-NBB}")], inc="sV", label=f"v_swt_{t}", key=kb(32.4))
        b.op("vector", lambda v, p=p: v.tensor_scalar(
                b_tot[p][:], b_tot[p][:], 1e-6, None, ALU.add),
             waits=[f"v_swt_{t}"], inc="sV", label=f"v_tot1_{t}", key=kb(33.0))
        b.op("vector", lambda v, p=p: v.reciprocal(b_ngm[p][:], b_tot[p][:]),
             waits=[f"v_tot1_{t}", f"v_gate_{t}"], inc="sV", label=f"v_rt_{t}", key=kb(33.2))
        b.op("vector", lambda v, p=p: v.tensor_scalar(
                b_sw[p][:, 0:6], b_sw[p][:, 0:6], b_ngm[p][:], None, ALU.mult),
             waits=[f"v_rt_{t}"], inc="sV", label=f"v_spw_{t}", key=kb(33.4))
        b.op("gpsimd", lambda g, p=p: g.tensor_tensor(
                b_cat2[p][:, 0:96].rearrange("t (e r) -> t e r", e=E),
                b_cat1[p][:, 0:96].rearrange("t (e r) -> t e r", e=E),
                b_sw[p][:, 0:6].unsqueeze(2).broadcast_to([128, E, R]), ALU.mult),
             waits=[f"v_spw_{t}", L(f"t_Tc2_{t-NBB}")], inc="sP", label=f"v_hsc_{t}", key=kb(33.6))
        b.op("scalar", lambda s, p=p: s.copy(
                b_cat2[p][:, 96:112], b_cat1[p][:, 96:112]),
             waits=[f"v_hsc_{t}", f"s_cat1_{t}"], inc="sS", label=f"v_shc_{t}", key=kb(33.8))

        b.op("tensor", lambda te, p=p, pT=pT: te.transpose(
                pT[0:CAT2, 0:128], b_cat2[p][:], s_id[:]),
             waits=[f"v_shc_{t}", f"s_u2fm_{t}"], inc="sT", label=f"t_Tc2_{t}", key=kb(34.5))
        b.op("vector", lambda v, p=p, pT=pT: v.tensor_scalar(
                b_c2fm[p][:], pT[0:CAT2, 0:128], 0.0, None, ALU.add),
             waits=[f"t_Tc2_{t}", L(f"t_ffn_{t-NBB}")], inc="sV", label=f"s_c2fm_{t}", key=kb(35.0))
        def mm2(te, p=p, pF=pF):
            for h in range(2):
                cs = slice(h * 512, (h + 1) * 512)
                ins = te.matmul(pF[:, cs], b_c2fm[p][:], s_w2[:, cs],
                                start=False, stop=(h == 1), skip_group_check=True)
            return ins
        b.op("tensor", mm2, waits=[f"s_c2fm_{t}", f"t_base_{t}", "w_r2s"],
             inc="sT", label=f"t_ffn_{t}", key=kb(35.5))
        b.op("vector", lambda v, p=p, pF=pF: v.bn_stats(b_st6[p][:, 0:6], pF[:, 0:512]),
             waits=[f"t_ffn_{t}", f"v_agg3_{t}"], inc="sV", key=kb(37.5))
        b.op("vector", lambda v, p=p, pF=pF: v.bn_stats(b_st6[p][:, 6:12], pF[:, 512:1024]),
             waits=[], inc="sV", label=f"v_bn4_{t}", key=kb(37.6))
        b.op("vector", lambda v, p=p: v.bn_aggr(b_st2[p][:], b_st6[p][:]),
             waits=[f"v_bn4_{t}", f"s_ln3_{t}"], inc="sV", label=f"v_agg4_{t}", key=kb(38.0))
        b.op("scalar", lambda s, p=p: s.activation(
                b_lnv[p][:], b_st2[p][:, 1:2], AF.Ln, bias=s_eps[:],
                scale=b_r3sq[p][:]),
             waits=[f"v_agg4_{t}", f"v_r3sq_{t}", f"s_sg3_{t}", f"v_lnr3_{t}"],
             inc="sS", label=f"s_ln4_{t}", key=kb(38.5))
        b.op("scalar", lambda s, p=p: s.activation(
                b_rstd[p][:], b_lnv[p][:], AF.Exp, scale=-0.5, bias=b_lnr3[p][:]),
             waits=[f"s_ln4_{t}", f"v_nmr2_{t}", f"s_ln2o_{t}"],
             inc="sS", label=f"s_rstd4_{t}", key=kb(38.8))
        b.op("vector", lambda v, p=p: v.tensor_scalar(
                b_nmr[p][:], b_st2[p][:, 0:1], b_rstd[p][:], -1.0, ALU.mult, ALU.mult),
             waits=[f"s_rstd4_{t}", f"s_ln2o_{t}"], inc="sV", label=f"v_nmr4_{t}", key=kb(39.1))
        b.op("scalar", lambda s, p=p, pF=pF: s.activation(
                b_z[p][:, 0:512], pF[:, 0:512], AF.Identity,
                bias=b_nmr[p][:], scale=b_rstd[p][:]),
             waits=[f"v_nmr4_{t}", f"po_x1_{t}"], inc="sS", label=f"s_ln4oa_{t}", key=kb(39.5))
        b.op("scalar", lambda s, p=p, pF=pF: s.activation(
                b_z[p][:, 512:1024], pF[:, 512:1024], AF.Identity,
                bias=b_nmr[p][:], scale=b_rstd[p][:]),
             waits=[], inc="sS", label=f"s_ln4o_{t}", key=kb(39.8))
        b.op("gpsimd", lambda g, p=p: g.tensor_add(
                b_x[p][:, 0:512], b_x1[p][:, 0:512], b_z[p][:, 0:512]),
             waits=[f"s_ln4oa_{t}"], inc="sP", label=f"po_x2a_{t}", key=kb(40.5))
        b.op("gpsimd", lambda g, p=p: g.tensor_add(
                b_x[p][:, 512:1024], b_x1[p][:, 512:1024], b_z[p][:, 512:1024]),
             waits=[f"s_ln4o_{t}"], inc="sP", label=f"po_x2_{t}", key=kb(41.0))
        b.op("gpsimd", lambda g, p=p, rs=rs: g.dma_start(out=y_out[rs, :], in_=b_x[p][:]),
             waits=[f"po_x2_{t}"], inc=f"sB_out{p}", n=16, label=f"g_out_{t}", key=kb(42.0))

    def nt_i(i):
        return len([t for t in range(NT) if t % NB == i])
    def nt_b(i):
        return len([t for t in range(NT) if t % NBB == i])
    retire = [("sCCd", 32), ("sCCg", NCORES * 16), ("sCC", 2),
              ("sWa", 16), ("sWid", 16), ("sWq", 16), ("sWk", 16), ("sWv", 16),
              ("sWqb", 16), ("sWkb", 16), ("sWvb", 16), ("sWffn", 80)]
    for i in range(NB):
        retire += [(f"sA_x{i}", nt_i(i) * 16), (f"sA_sp{i}", nt_i(i) * 48)]
    for i in range(NBB):
        retire += [(f"sB_out{i}", nt_b(i) * 16), (f"sB_in{i}", nt_b(i) * 48),
                   (f"sB_v{i}", nt_b(i) * 16)]
    b.op("gpsimd", lambda g: g.nop(), waits=retire, key=1e9)

    b.resolve()
    with nc.Block() as block:
        def mkbody(ename):
            def body(eng):
                last = {}
                for r in b.ops[ename]:
                    for semn, val in r["rw"]:
                        if last.get(semn, 0) < val:
                            eng.wait_ge(sems[semn], val)
                            last[semn] = val
                    ins = r["fn"](eng)
                    if r["inc"] is not None:
                        ins.then_inc(sems[r["inc"]], r["n"])
            return body
        block.sync(mkbody("sync"))
        block.gpsimd(mkbody("gpsimd"))
        block.vector(mkbody("vector"))
        block.scalar(mkbody("scalar"))
        block.tensor(mkbody("tensor"))

    es.close()
    return nc


def _host_layout(inputs):
    f32 = np.float32
    Wq = np.asarray(inputs["L0_attn_Wq"], f32)
    Wk = np.asarray(inputs["L0_attn_Wk"], f32)
    Wv = np.asarray(inputs["L0_attn_Wv"], f32)
    Wb = np.asarray(inputs["L0_ffn_base_W"], f32)
    weights = np.asarray(inputs["L0_attn_weights"], f32)
    shared_A = np.asarray(inputs["L0_ffn_shared_A"], f32)
    shared_B = np.asarray(inputs["L0_ffn_shared_B"], f32)
    routed_A = np.asarray(inputs["L0_ffn_routed_A"], f32)
    routed_B = np.asarray(inputs["L0_ffn_routed_B"], f32)
    router_W = np.asarray(inputs["L0_ffn_router_W"], f32)
    router_b = np.asarray(inputs["L0_ffn_router_b"], f32)

    wm = {}
    wm["wqT"] = np.ascontiguousarray(Wq.T)
    wm["wkT"] = np.ascontiguousarray(Wk.T)
    wm["wvT"] = np.ascontiguousarray(Wv.T)
    wm["wbT"] = np.ascontiguousarray(Wb.T)
    bd = np.zeros((C, 128, 128), f32)
    for c in range(C):
        bd[c, 0:64, 0:64] = weights[2 * c].T
        bd[c, 64:128, 64:128] = weights[2 * c + 1].T
    wm["wbd"] = bd
    wsmall = np.concatenate([routed_A.reshape(96, D), shared_A, router_W], 0)
    wm["wsT"] = np.ascontiguousarray(wsmall.T)
    wm["w2"] = np.ascontiguousarray(
        np.concatenate([routed_B.transpose(0, 2, 1).reshape(96, D), shared_B.T], 0))
    wm["bqr"] = np.ascontiguousarray(inputs["L0_attn_bq"].astype(f32)[None, :])
    wm["bkr"] = np.ascontiguousarray(inputs["L0_attn_bk"].astype(f32)[None, :])
    wm["bvr"] = np.ascontiguousarray(inputs["L0_attn_bv"].astype(f32)[None, :])
    wm["bbr"] = np.ascontiguousarray(inputs["L0_ffn_base_b"].astype(f32)[None, :])
    wm["b118"] = np.ascontiguousarray(
        np.concatenate([np.zeros(CAT2, f32), router_b])[None, :])
    wm["identd"] = np.eye(128, dtype=f32)
    return wm


def kernel(**inputs):
    x = np.asarray(inputs["L0_attn_x"], np.float32)
    wm = _host_layout(inputs)
    nc = build_program(32)
    in_maps = [dict(wm, x=np.ascontiguousarray(x[c])) for c in range(NCORES)]
    res = run_bass_kernel_spmd(nc, in_maps, core_ids=list(range(NCORES)))
    return np.stack([res.results[c]["y"] for c in range(NCORES)], axis=0)
